# version 01
# baseline (speedup 1.0000x reference)
import numpy as np
from contextlib import ExitStack

import concourse.bass as bass
import concourse.mybir as mybir
from concourse.bass_utils import run_bass_kernel_spmd

F32 = mybir.dt.float32
ALU = mybir.AluOpType
AF = mybir.ActivationFunctionType
AX = mybir.AxisListType

CH = 256
NCH = 16
OWN0 = 12
H = 64
HD = 64
DM = 2048
FD = DM // 128
DIN = 4096
ZCC = DIN // 128
CONVCC = 34
NEXP = 32
INTER = 768
JCC = INTER // 128
TOK = 1024

ENG_METHOD = {"te": "tensor", "ve": "vector", "sc": "scalar",
              "gp": "gpsimd", "sy": "sync"}

TAPS = False


class Res:
    __slots__ = ("w", "r")

    def __init__(self):
        self.w = None
        self.r = []


class Prog:
    def __init__(self, nc, ctx):
        self.nc = nc
        self.ctx = ctx
        self.q = {k: [] for k in ENG_METHOD}
        self.sem = {}
        self.cnt = {}
        for k in ENG_METHOD:
            self.sem[k] = ctx.enter_context(nc.semaphore("sem_" + k))
            self.cnt[k] = 0
        self.lastwait = {}
        self.fence_state = None
        self.fence_absorbed = set()

    def dsem(self, name):
        key = "d_" + name
        if key not in self.sem:
            self.sem[key] = self.ctx.enter_context(
                self.nc.semaphore("sem" + key))
            self.cnt[key] = 0
        return key

    def _wait(self, en, dep):
        sem, val = dep
        if sem.startswith("d_"):
            val = max(val, self.cnt[sem])

        key = (en, sem)
        if self.lastwait.get(key, 0) >= val:
            return
        self.lastwait[key] = val
        s = self.sem[sem]
        self.q[en].append(lambda e, s=s, v=val: e.wait_ge(s, v))

    def op(self, en, build, reads=(), writes=(), rmw=(), dma=False):
        if self.fence_state is not None and en not in self.fence_absorbed:
            self.fence_absorbed.add(en)
            for sem, val in self.fence_state.items():
                if val > 0:
                    self._wait(en, (sem, val))
        deps = []
        for r in reads:
            if r.w:
                deps.append(r.w)
        for r in tuple(writes) + tuple(rmw):
            if r.w:
                deps.append(r.w)
            deps.extend(r.r)
        byname = {}
        for s, v in deps:
            if byname.get(s, 0) < v:
                byname[s] = v
        for s, v in byname.items():
            self._wait(en, (s, v))
        builders = build if isinstance(build, (list, tuple)) else [build]
        semn = self.dsem(dma) if dma else en
        s = self.sem[semn]
        if dma:
            for b in builders:
                self.cnt[semn] += 16
                self.q[en].append(lambda e, b=b, s=s: b(e).then_inc(s, 16))
        else:
            for b in builders[:-1]:
                self.q[en].append(lambda e, b=b: b(e))
            self.cnt[semn] += 1
            self.q[en].append(
                lambda e, b=builders[-1], s=s: b(e).then_inc(s, 1))
        tok = (semn, self.cnt[semn])
        for r in reads:
            rr = [t for t in r.r if t[0] != tok[0]]
            rr.append(tok)
            r.r = rr
        for r in tuple(writes) + tuple(rmw):
            r.w = tok
            r.r = []
        return tok

    def fence(self):
        self.fence_state = dict(self.cnt)
        self.fence_absorbed = set()

    def raw(self, en, closure):
        self.q[en].append(closure)

    def flush(self, block):
        for en, meth in ENG_METHOD.items():
            lst = self.q[en]
            if not lst:
                continue

            def runner(e, lst=lst):
                for f in lst:
                    f(e)
            getattr(block, meth)(runner)
            self.q[en] = []


def build_program():
    nc = bass.Bass("TRN2", target_bir_lowering=False)

    d_uT = nc.dram_tensor("uT", [DM, NCH * CH], F32, kind="ExternalInput").ap()
    d_win = nc.dram_tensor("w_in", [DM, 8512], F32, kind="ExternalInput").ap()
    d_wout = nc.dram_tensor("w_out", [DIN, DM], F32, kind="ExternalInput").ap()
    d_convw = nc.dram_tensor("convw", [128, CONVCC * 4], F32,
                             kind="ExternalInput").ap()
    d_convb = nc.dram_tensor("convb", [128, CONVCC], F32,
                             kind="ExternalInput").ap()
    d_dtb = nc.dram_tensor("dtb", [64, 1], F32, kind="ExternalInput").ap()
    d_alog = nc.dram_tensor("alog", [64, 1], F32, kind="ExternalInput").ap()
    d_dpf = nc.dram_tensor("dpf", [128, H], F32, kind="ExternalInput").ap()
    d_normw = nc.dram_tensor("normw", [128, ZCC], F32,
                             kind="ExternalInput").ap()
    d_gwT = nc.dram_tensor("gwT", [DM, NEXP], F32, kind="ExternalInput").ap()
    d_wgu = nc.dram_tensor("wgu", [NEXP, DM, 1536], F32,
                           kind="ExternalInput").ap()
    d_wd = nc.dram_tensor("wd", [NEXP, INTER, DM], F32,
                          kind="ExternalInput").ap()
    d_valid = nc.dram_tensor("valid", [128, NCH], F32,
                             kind="ExternalInput").ap()
    d_inval = nc.dram_tensor("invalid", [128, NCH], F32,
                             kind="ExternalInput").ap()
    d_out = nc.dram_tensor("out", [DM, TOK], F32, kind="ExternalOutput").ap()
    d_taps = {}
    if TAPS:
        for ci in range(OWN0, NCH):
            d_taps["S%d" % ci] = nc.dram_tensor(
                "tapS%d" % ci, [128, DIN], F32, kind="ExternalOutput").ap()
            d_taps["e%d" % ci] = nc.dram_tensor(
                "tape%d" % ci, [128, 64], F32, kind="ExternalOutput").ap()

    winr = d_win.rearrange("(k p) m -> p k m", p=128)
    woutr = d_wout.rearrange("(k p) m -> p k m", p=128)
    gwTr = d_gwT.rearrange("(k p) m -> p k m", p=128)

    with ExitStack() as ctx:
        p = Prog(nc, ctx)

        def sb(name, cols, rows=128):
            t = ctx.enter_context(nc.sbuf_tensor(name, [rows, cols], F32))
            return t, Res()

        ones, r_ones = sb("ones", 256)
        id128, r_id = sb("id128", 128)
        pen, r_pen = sb("pen", 2 * CH)
        identD, r_identD = sb("identD", 2 * CH)
        iotam, r_iotam = sb("iotam", 32)
        csm, r_csm = sb("csm", 8)
        valid, r_valid = sb("validsb", NCH)
        inval, r_inval = sb("invalsb", NCH)
        dtb, r_dtb = sb("dtbsb", 1, rows=64)
        alog, r_alog = sb("alogsb", 1, rows=64)
        acol, r_acol = sb("acolsb", 1, rows=64)
        dpf, r_dpf = sb("dpfsb", H)
        normw, r_normw = sb("normwsb", ZCC)
        convw, r_convw = sb("convwsb", CONVCC * 4)
        convb, r_convb = sb("convbsb", CONVCC)
        gwT, r_gwT = sb("gwTsb", FD * NEXP)
        rows, r_rows = sb("rowsbuf", 1280)
        ysq, r_ysq = sb("ysqbuf", CH)
        ebT, r_ebT = sb("ebTsb", H)
        erow, r_erow = sb("erowsb", H)
        ecol, r_ecol = sb("ecolsb", 2, rows=64)

        pb = []
        pbres = []
        for i in range(8):
            t = ctx.enter_context(nc.psum_tensor("pb%d" % i, [128, 512], F32))
            pb.append(t)
            pbres.append(Res())

        AR_COLS = 47760
        arena = ctx.enter_context(nc.sbuf_tensor("arena", [128, AR_COLS],
                                                 F32))

        def seg(off, cols, rows=128):
            return arena[0:rows, off:off + cols]

        o = 0

        def nxt(cols):
            nonlocal o
            a = o
            o += cols
            return a
        O_UC = nxt(FD * CH)
        O_XNC = nxt(FD * CH)
        O_RAWXT = nxt(CONVCC * 259)
        O_XBC = nxt(CONVCC * CH)
        O_HALO = nxt(CONVCC * 3)
        O_DTS = nxt(6 * CH)
        O_TTS = nxt(3 * 128)
        O_BT = nxt(2 * 128)
        O_G2 = nxt(2 * CH)
        O_MH = nxt(2 * CH)
        O_M0 = nxt(2 * CH)
        O_TT = nxt(2 * CH)
        O_CX = nxt(2 * CH)
        O_ET1 = nxt(CH)
        O_CONVT = nxt(2 * CH)
        O_BS = nxt(2 * 128)
        O_WBUF = nxt(2 * FD * 128)
        O_S = nxt(DIN)
        assert o <= AR_COLS, o

        u_c = seg(O_UC, FD * CH)
        xn_c = seg(O_XNC, FD * CH)
        rawXT = seg(O_RAWXT, CONVCC * 259)
        xbc = seg(O_XBC, CONVCC * CH)
        halob = seg(O_HALO, CONVCC * 3)
        dts = seg(O_DTS, 6 * CH, rows=64)
        tts = seg(O_TTS, 3 * 128)
        BT = seg(O_BT, 2 * 128)
        G2sb = seg(O_G2, 2 * CH)
        mh = seg(O_MH, 2 * CH)
        m0 = seg(O_M0, 2 * CH)
        ttmp = seg(O_TT, 2 * CH)
        cexp = seg(O_CX, 2 * CH)
        expT1 = seg(O_ET1, CH)
        convt = seg(O_CONVT, 2 * CH)
        Bs = seg(O_BS, 2 * 128)
        wbuf = seg(O_WBUF, 2 * FD * 128)
        S = seg(O_S, DIN)

        r_uc, r_xnc = Res(), Res()
        r_raw = [Res() for _ in range(CONVCC)]
        r_XT = Res()
        r_xbcx, r_xbcBC = Res(), Res()
        r_halob = Res()
        r_dtraw, r_dt, r_dA, r_acum, r_dtds = (Res() for _ in range(5))
        r_nAT, r_dtT, r_dtdsT = Res(), Res(), Res()
        r_BT, r_G2, r_mh, r_expT1 = Res(), Res(), Res(), Res()
        r_m0 = [Res(), Res()]
        r_ttmp = [Res(), Res()]
        r_cexp = [Res(), Res()]
        r_convt = [Res(), Res()]
        r_Bs = [Res(), Res()]
        r_wbuf = [Res(), Res()]
        r_S = Res()

        for dst, src, res in [(valid, d_valid, r_valid),
                              (inval, d_inval, r_inval),
                              (dtb, d_dtb, r_dtb), (alog, d_alog, r_alog),
                              (dpf, d_dpf, r_dpf), (normw, d_normw, r_normw),
                              (convw, d_convw, r_convw),
                              (convb, d_convb, r_convb)]:
            p.op("sy", lambda e, dst=dst, src=src: e.dma_start(
                out=dst[:, :], in_=src), writes=[res], dma="cst")
        p.op("sy", lambda e: e.dma_start(
            out=gwT[:, :].rearrange("p (k m) -> p k m", k=FD), in_=gwTr),
            writes=[r_gwT], dma="cst")

        p.op("ve", lambda e: e.memset(ones[:, :], 1.0), writes=[r_ones])
        p.op("ve", lambda e: e.memset(csm[:, 0:1], 1.0), writes=[r_csm])
        p.op("ve", lambda e: e.memset(csm[:, 1:2], 1e-5), rmw=[r_csm])
        p.op("gp", lambda e: e.iota(id128[:, :], pattern=[[1, 128]], base=0,
                                    channel_multiplier=-1,
                                    allow_small_or_imprecise_dtypes=True),
             writes=[r_id])
        p.op("gp", lambda e: e.tensor_scalar(out=id128[:, :], in0=id128[:, :],
                                             scalar1=0.0, scalar2=None,
                                             op0=ALU.is_equal), rmw=[r_id])
        for k in range(2):
            p.op("gp", lambda e, k=k: e.iota(
                pen[:, k * CH:(k + 1) * CH], pattern=[[1, CH]], base=-128 * k,
                channel_multiplier=-1, allow_small_or_imprecise_dtypes=True),
                rmw=[r_pen])
            p.op("gp", lambda e, k=k: e.tensor_scalar(
                out=identD[:, k * CH:(k + 1) * CH],
                in0=pen[:, k * CH:(k + 1) * CH],
                scalar1=0.0, scalar2=None, op0=ALU.is_equal),
                reads=[r_pen], rmw=[r_identD])
            p.op("gp", lambda e, k=k: e.tensor_scalar(
                out=pen[:, k * CH:(k + 1) * CH],
                in0=pen[:, k * CH:(k + 1) * CH],
                scalar1=0.0, scalar2=-1e30, op0=ALU.is_lt, op1=ALU.mult),
                rmw=[r_pen])
        p.op("gp", lambda e: e.iota(iotam[:, :], pattern=[[1, 32]], base=-999,
                                    channel_multiplier=0,
                                    allow_small_or_imprecise_dtypes=True),
             writes=[r_iotam])
        p.op("sc", lambda e: e.activation(out=acol[:, :], in_=alog[:, :],
                                          func=AF.Exp),
             reads=[r_alog], writes=[r_acol])
        p.op("ve", lambda e: e.tensor_scalar(out=acol[:, :], in0=acol[:, :],
                                             scalar1=-1.0, scalar2=None,
                                             op0=ALU.mult), rmw=[r_acol])
        p.op("ve", lambda e: e.memset(S[:, :], 0.0), writes=[r_S])
        p.op("ve", lambda e: e.memset(halob[:, :], 0.0), writes=[r_halob])

        def emit_ln(ci, xn_ap_fn, r_xn_dest, u_buf, r_ubuf, usem="u"):
            dmas = [lambda e, f=f, ci=ci: e.dma_start(
                out=u_buf[:, f * CH:(f + 1) * CH],
                in_=d_uT[f * 128:(f + 1) * 128, ci * CH:(ci + 1) * CH])
                for f in range(FD)]
            p.op("sy", dmas, writes=[r_ubuf], dma=usem)
            for f in range(FD):
                p.op("sc", lambda e, f=f: e.square(
                    out=ysq[:, :], in_=u_buf[:, f * CH:(f + 1) * CH]),
                    reads=[r_ubuf], writes=[r_ysq])
                p.op("te", lambda e, f=f: e.matmul(
                    pb[6][0:1, 0:CH], ones[:, 0:1],
                    u_buf[:, f * CH:(f + 1) * CH],
                    start=(f == 0), stop=(f == FD - 1)),
                    reads=[r_ubuf, r_ones],
                    writes=[pbres[6]] if f == 0 else (),
                    rmw=[pbres[6]] if f else ())
                p.op("te", lambda e, f=f: e.matmul(
                    pb[7][0:1, 0:CH], ones[:, 0:1], ysq[:, :],
                    start=(f == 0), stop=(f == FD - 1)),
                    reads=[r_ysq, r_ones],
                    writes=[pbres[7]] if f == 0 else (),
                    rmw=[pbres[7]] if f else ())
            p.op("sc", lambda e: e.copy(out=rows[0:1, 0:CH],
                                        in_=pb[6][0:1, 0:CH]),
                 reads=[pbres[6]], rmw=[r_rows])
            p.op("sc", lambda e: e.copy(out=rows[0:1, CH:2 * CH],
                                        in_=pb[7][0:1, 0:CH]),
                 reads=[pbres[7]], rmw=[r_rows])
            p.op("ve", lambda e: e.tensor_scalar(
                out=rows[0:1, 2 * CH:3 * CH], in0=rows[0:1, 0:CH],
                scalar1=1.0 / DM, scalar2=None, op0=ALU.mult), rmw=[r_rows])
            p.op("ve", lambda e: e.tensor_scalar(
                out=rows[0:1, CH:2 * CH], in0=rows[0:1, CH:2 * CH],
                scalar1=1.0 / DM, scalar2=None, op0=ALU.mult), rmw=[r_rows])
            p.op("ve", lambda e: e.tensor_tensor(
                out=rows[0:1, 0:CH], in0=rows[0:1, 2 * CH:3 * CH],
                in1=rows[0:1, 2 * CH:3 * CH], op=ALU.mult), rmw=[r_rows])
            p.op("ve", lambda e: e.tensor_tensor(
                out=rows[0:1, 0:CH], in0=rows[0:1, CH:2 * CH],
                in1=rows[0:1, 0:CH], op=ALU.subtract), rmw=[r_rows])
            p.op("sc", lambda e: e.activation(
                out=rows[0:1, 4 * CH:5 * CH], in_=rows[0:1, 0:CH],
                func=AF.Sqrt, bias=csm[0:1, 1:2], scale=1.0),
                reads=[r_csm], rmw=[r_rows])
            p.op("ve", lambda e: e.reciprocal(
                out=rows[0:1, 3 * CH:4 * CH], in_=rows[0:1, 4 * CH:5 * CH]),
                rmw=[r_rows])
            p.op("te", lambda e: e.matmul(
                pb[6][:, 0:CH], ones[0:1, 0:128], rows[0:1, 2 * CH:3 * CH],
                start=True, stop=True),
                reads=[r_rows, r_ones], writes=[pbres[6]])
            p.op("te", lambda e: e.matmul(
                pb[7][:, 0:CH], ones[0:1, 0:128], rows[0:1, 3 * CH:4 * CH],
                start=True, stop=True),
                reads=[r_rows, r_ones], writes=[pbres[7]])
            for f in range(FD):
                ap = xn_ap_fn(f)
                p.op("ve", lambda e, f=f, ap=ap: e.scalar_tensor_tensor(
                    out=ap, in0=pb[6][:, 0:CH], scalar=-1.0,
                    in1=u_buf[:, f * CH:(f + 1) * CH],
                    op0=ALU.mult, op1=ALU.add),
                    reads=[pbres[6], r_ubuf],
                    writes=[r_xn_dest] if f == 0 else (),
                    rmw=[r_xn_dest] if f else ())
                p.op("ve", lambda e, ap=ap: e.tensor_tensor(
                    out=ap, in0=ap, in1=pb[7][:, 0:CH], op=ALU.mult),
                    reads=[pbres[7]], rmw=[r_xn_dest])

        def load_w_slice(dst_off, src_ap, res, nk, width, buf, sem):
            p.op("sy", lambda e: e.dma_start(
                out=buf[:, dst_off:dst_off + nk * width].rearrange(
                    "p (k m) -> p k m", k=nk),
                in_=src_ap), writes=[res], dma=sem)

        def emit_inproj(ci, mlist, rhs_ap, r_rhs):
            for i, (c0, width, kind, idx) in enumerate(mlist):
                wb = i % 2
                bank = i % 2
                load_w_slice(wb * FD * 128, winr[:, :, c0:c0 + width],
                             r_wbuf[wb], FD, width, wbuf, "w%d" % wb)
                mm = [lambda e, k=k, wb=wb, bank=bank, width=width: e.matmul(
                    pb[bank][0:width, 0:CH],
                    wbuf[:, wb * FD * 128 + k * width:
                         wb * FD * 128 + (k + 1) * width],
                    rhs_ap[:, k * CH:(k + 1) * CH],
                    start=(k == 0), stop=(k == FD - 1)) for k in range(FD)]
                p.op("te", mm, reads=[r_wbuf[wb], r_rhs],
                     writes=[pbres[bank]])
                if kind == "raw":
                    cc = idx
                    p.op("ve", lambda e, cc=cc: e.tensor_copy(
                        out=rawXT[:, cc * 259:cc * 259 + 3],
                        in_=halob[:, cc * 3:cc * 3 + 3]),
                        reads=[r_halob], writes=[r_raw[cc]], rmw=[r_XT])
                    p.op("sc", lambda e, cc=cc, bank=bank: e.copy(
                        out=rawXT[:, cc * 259 + 3:cc * 259 + 259],
                        in_=pb[bank][:, 0:CH]),
                        reads=[pbres[bank]], rmw=[r_raw[cc], r_XT])
                elif kind == "dt":
                    p.op("sc", lambda e, bank=bank: e.copy(
                        out=dts[:, 0:CH], in_=pb[bank][0:64, 0:CH]),
                        reads=[pbres[bank]], writes=[r_dtraw])
                elif kind == "z":
                    zc = idx
                    p.op("sc", lambda e, zc=zc, bank=bank: e.activation(
                        out=xbc[:, zc * CH:(zc + 1) * CH],
                        in_=pb[bank][:, 0:CH], func=AF.Silu),
                        reads=[pbres[bank]],
                        writes=[r_xbcx] if zc == 0 else (),
                        rmw=[r_xbcx] if zc else ())

        def emit_conv(ci, ncc):
            for cc in range(ncc):
                ct = cc % 2
                p.op("ve", lambda e, cc=cc, ct=ct: e.tensor_scalar(
                    out=convt[:, ct * CH:(ct + 1) * CH],
                    in0=rawXT[:, cc * 259 + 0:cc * 259 + CH],
                    scalar1=convw[:, cc * 4:cc * 4 + 1], scalar2=None,
                    op0=ALU.mult),
                    reads=[r_raw[cc], r_convw], writes=[r_convt[ct]])
                for k in range(1, 4):
                    p.op("ve", lambda e, cc=cc, k=k, ct=ct:
                         e.scalar_tensor_tensor(
                        out=convt[:, ct * CH:(ct + 1) * CH],
                        in0=rawXT[:, cc * 259 + k:cc * 259 + k + CH],
                        scalar=convw[:, cc * 4 + k:cc * 4 + k + 1],
                        in1=convt[:, ct * CH:(ct + 1) * CH],
                        op0=ALU.mult, op1=ALU.add),
                        reads=[r_raw[cc], r_convw], rmw=[r_convt[ct]])
                tgt = (r_xbcx if cc < 32 else r_xbcBC)
                p.op("sc", lambda e, cc=cc, ct=ct: e.activation(
                    out=xbc[:, cc * CH:(cc + 1) * CH],
                    in_=convt[:, ct * CH:(ct + 1) * CH], func=AF.Silu,
                    bias=convb[:, cc:cc + 1], scale=1.0),
                    reads=[r_convt[ct], r_convb],
                    writes=[tgt] if cc in (0, 32) else (),
                    rmw=[tgt] if cc not in (0, 32) else ())
            if ci < NCH - 1:
                for cc in range(ncc):
                    p.op("ve", lambda e, cc=cc: e.tensor_copy(
                        out=halob[:, cc * 3:cc * 3 + 3],
                        in_=rawXT[:, cc * 259 + CH:cc * 259 + CH + 3]),
                        reads=[r_raw[cc]], rmw=[r_halob])

        def emit_dt(ci, own):
            p.op("sc", lambda e: e.activation(
                out=dts[:, CH:2 * CH], in_=dts[:, 0:CH], func=AF.Exp,
                bias=dtb[:, 0:1], scale=1.0),
                reads=[r_dtraw, r_dtb], writes=[r_dt])
            p.op("sc", lambda e: e.activation(
                out=dts[:, 5 * CH:6 * CH], in_=dts[:, CH:2 * CH], func=AF.Ln,
                bias=csm[0:64, 0:1], scale=1.0),
                reads=[r_csm], rmw=[r_dt])
            p.op("ve", lambda e: e.tensor_scalar(
                out=dts[:, 2 * CH:3 * CH], in0=dts[:, 5 * CH:6 * CH],
                scalar1=acol[:, 0:1], scalar2=None, op0=ALU.mult),
                reads=[r_dt, r_acol], writes=[r_dA])
            p.op("ve", lambda e: e.tensor_tensor_scan(
                out=dts[:, 3 * CH:4 * CH], data0=ones[0:64, 0:CH],
                data1=dts[:, 2 * CH:3 * CH], initial=0.0,
                op0=ALU.mult, op1=ALU.add),
                reads=[r_dA, r_ones], writes=[r_acum])
            p.op("sc", lambda e: e.activation(
                out=ecol[:, 0:1], in_=dts[:, 4 * CH - 1:4 * CH],
                func=AF.Exp), reads=[r_acum], writes=[r_ecol])
            p.op("ve", lambda e, ci=ci: e.scalar_tensor_tensor(
                out=ecol[:, 1:2], in0=ecol[:, 0:1],
                scalar=valid[0:64, ci:ci + 1], in1=inval[0:64, ci:ci + 1],
                op0=ALU.mult, op1=ALU.add),
                reads=[r_valid, r_inval], rmw=[r_ecol])
            p.op("sc", lambda e: e.activation(
                out=dts[:, 4 * CH:5 * CH], in_=dts[:, 3 * CH:4 * CH],
                func=AF.Exp, bias=dts[:, 4 * CH - 1:4 * CH], scale=-1.0),
                reads=[r_acum], writes=[r_dtds])
            p.op("ve", lambda e: e.tensor_tensor(
                out=dts[:, 4 * CH:5 * CH], in0=dts[:, 4 * CH:5 * CH],
                in1=dts[:, 5 * CH:6 * CH], op=ALU.mult),
                reads=[r_dt], rmw=[r_dtds])

        def emit_transposes(ci, own):
            for k in range(2):
                bank = 2 + k
                p.op("te", lambda e, k=k, bank=bank: e.transpose(
                    pb[bank][:, 0:128],
                    xbc[:, 32 * CH + k * 128:32 * CH + (k + 1) * 128],
                    id128[:, :]),
                    reads=[r_xbcBC, r_id], writes=[pbres[bank]])
                p.op("sc", lambda e, k=k, bank=bank: e.copy(
                    out=BT[:, k * 128:(k + 1) * 128], in_=pb[bank][:, 0:128]),
                    reads=[pbres[bank]],
                    writes=[r_BT] if k == 0 else (), rmw=[r_BT] if k else ())
            specs = [(4, r_dtds, 256, 1.0, r_dtdsT)]
            if own:
                specs.append((3, r_acum, 0, -1.0, r_nAT))
                specs.append((5, r_dt, 128, 1.0, r_dtT))
            for (so, rsrc, do, scl, rdst) in specs:
                for k in range(2):
                    bank = 2 + k
                    p.op("te", lambda e, so=so, k=k, bank=bank: e.transpose(
                        pb[bank][:, 0:64],
                        dts[:, so * CH + k * 128:so * CH + (k + 1) * 128],
                        id128[0:64, 0:64]),
                        reads=[rsrc, r_id], writes=[pbres[bank]])
                    if scl == 1.0:
                        p.op("sc", lambda e, do=do, k=k, bank=bank: e.copy(
                            out=tts[:, do + k * 64:do + (k + 1) * 64],
                            in_=pb[bank][:, 0:64]),
                            reads=[pbres[bank]],
                            writes=[rdst] if k == 0 else (),
                            rmw=[rdst] if k else ())
                    else:
                        p.op("ve", lambda e, do=do, k=k, bank=bank:
                             e.tensor_scalar(
                            out=tts[:, do + k * 64:do + (k + 1) * 64],
                            in0=pb[bank][:, 0:64], scalar1=-1.0, scalar2=None,
                            op0=ALU.mult),
                            reads=[pbres[bank]],
                            writes=[rdst] if k == 0 else (),
                            rmw=[rdst] if k else ())
            first = [True]
            for f in range(ZCC):
                for k in range(2):
                    bank = 2 + ((f * 2 + k) % 2)
                    p.op("te", lambda e, f=f, k=k, bank=bank: e.transpose(
                        pb[bank][:, 0:128],
                        xbc[:, f * CH + k * 128:f * CH + (k + 1) * 128],
                        id128[:, :]),
                        reads=[r_xbcx, r_id], writes=[pbres[bank]])
                    if first[0]:
                        first[0] = False
                        wl, rl = [r_XT] + r_raw, ()
                    else:
                        wl, rl = (), [r_XT]
                    p.op("sc", lambda e, f=f, k=k, bank=bank: e.copy(
                        out=rawXT[:, k * DIN + f * 128:k * DIN + (f + 1) * 128],
                        in_=pb[bank][:, 0:128]),
                        reads=[pbres[bank]], writes=wl, rmw=rl)

        def emit_G2(ci):
            mm = [lambda e, k=k: e.matmul(
                pb[4][:, k * CH:(k + 1) * CH],
                xbc[:, 32 * CH + k * 128:32 * CH + (k + 1) * 128],
                xbc[:, 33 * CH:34 * CH], start=True, stop=True)
                for k in range(2)]
            p.op("te", mm, reads=[r_xbcBC], writes=[pbres[4]])
            for k in range(2):
                p.op("ve", lambda e, k=k: e.tensor_copy(
                    out=G2sb[:, k * CH:(k + 1) * CH],
                    in_=pb[4][:, k * CH:(k + 1) * CH]),
                    reads=[pbres[4]],
                    writes=[r_G2] if k == 0 else (), rmw=[r_G2] if k else ())

        def emit_Y(ci):
            for h in range(H):
                p.op("te", lambda e, h=h: e.matmul(
                    pb[5][:, 0:CH],
                    id128[0:64, h:h + 1].to_broadcast([64, 128]),
                    dts[:, 3 * CH:4 * CH], start=True, stop=True),
                    reads=[r_acum, r_id], writes=[pbres[5]])
                p.op("sc", lambda e: e.activation(
                    out=expT1[:, :], in_=pb[5][:, 0:CH], func=AF.Exp),
                    reads=[pbres[5]], writes=[r_expT1])
                cx = h % 2
                p.op("ve", lambda e, cx=cx: e.tensor_tensor(
                    out=cexp[:, cx * CH:(cx + 1) * CH],
                    in0=xbc[:, 33 * CH:34 * CH], in1=expT1[:, :],
                    op=ALU.mult),
                    reads=[r_xbcBC, r_expT1], writes=[r_cexp[cx]])
                for k in range(2):
                    p.op("ve", lambda e, k=k: e.tensor_tensor(
                        out=ttmp[:, k * CH:(k + 1) * CH],
                        in0=pb[5][:, 0:CH], in1=pen[:, k * CH:(k + 1) * CH],
                        op=ALU.add),
                        reads=[pbres[5], r_pen], writes=[r_ttmp[k]])
                    p.op("sc", lambda e, k=k, h=h: e.activation(
                        out=m0[:, k * CH:(k + 1) * CH],
                        in_=ttmp[:, k * CH:(k + 1) * CH], func=AF.Exp,
                        bias=tts[:, k * 64 + h:k * 64 + h + 1], scale=1.0),
                        reads=[r_ttmp[k], r_nAT], writes=[r_m0[k]])
                    p.op("ve", lambda e, k=k, h=h: e.scalar_tensor_tensor(
                        out=mh[:, k * CH:(k + 1) * CH],
                        in0=G2sb[:, k * CH:(k + 1) * CH],
                        scalar=tts[:, 128 + k * 64 + h:128 + k * 64 + h + 1],
                        in1=m0[:, k * CH:(k + 1) * CH],
                        op0=ALU.mult, op1=ALU.mult),
                        reads=[r_G2, r_dtT, r_m0[k]],
                        writes=[r_mh] if k == 0 else (),
                        rmw=[r_mh] if k else ())
                    p.op("ve", lambda e, k=k, h=h: e.scalar_tensor_tensor(
                        out=mh[:, k * CH:(k + 1) * CH],
                        in0=identD[:, k * CH:(k + 1) * CH],
                        scalar=dpf[:, h:h + 1],
                        in1=mh[:, k * CH:(k + 1) * CH],
                        op0=ALU.mult, op1=ALU.add),
                        reads=[r_identD, r_dpf], rmw=[r_mh])
                pairbank = 2 + ((h // 2) % 2)
                rb = (h % 2) * 64
                mm = [lambda e, h=h, k=k, pairbank=pairbank, rb=rb: e.matmul(
                    pb[pairbank][rb:rb + 64, 0:CH],
                    rawXT[:, k * DIN + h * HD:k * DIN + (h + 1) * HD],
                    mh[:, k * CH:(k + 1) * CH],
                    start=(k == 0), stop=False) for k in range(2)]
                mm.append(lambda e, h=h, pairbank=pairbank, rb=rb, cx=cx:
                          e.matmul(
                    pb[pairbank][rb:rb + 64, 0:CH],
                    S[:, h * HD:(h + 1) * HD],
                    cexp[:, cx * CH:(cx + 1) * CH],
                    start=False, stop=True))
                p.op("te", mm, reads=[r_XT, r_mh, r_S, r_cexp[cx]],
                     writes=[pbres[pairbank]] if h % 2 == 0 else (),
                     rmw=[pbres[pairbank]] if h % 2 else ())
                if h % 2 == 1:
                    pair = h // 2
                    p.op("ve", lambda e, pair=pair, pairbank=pairbank:
                         e.tensor_tensor(
                        out=xbc[:, pair * CH:(pair + 1) * CH],
                        in0=pb[pairbank][:, 0:CH],
                        in1=xbc[:, pair * CH:(pair + 1) * CH], op=ALU.mult),
                        reads=[pbres[pairbank]], rmw=[r_xbcx])
                    p.op("sc", lambda e, pair=pair: e.square(
                        out=ysq[:, :], in_=xbc[:, pair * CH:(pair + 1) * CH]),
                        reads=[r_xbcx], writes=[r_ysq])
                    p.op("te", lambda e, pair=pair: e.matmul(
                        pb[6][0:1, 0:CH], ones[:, 0:1], ysq[:, :],
                        start=(pair == 0), stop=(pair == 31)),
                        reads=[r_ysq, r_ones],
                        writes=[pbres[6]] if pair == 0 else (),
                        rmw=[pbres[6]] if pair else ())

        def emit_rms_outproj(ci):
            p.op("sc", lambda e: e.copy(out=rows[0:1, 0:CH],
                                        in_=pb[6][0:1, 0:CH]),
                 reads=[pbres[6]], rmw=[r_rows])
            p.op("ve", lambda e: e.tensor_scalar(
                out=rows[0:1, 0:CH], in0=rows[0:1, 0:CH],
                scalar1=1.0 / DIN, scalar2=None, op0=ALU.mult), rmw=[r_rows])
            p.op("sc", lambda e: e.activation(
                out=rows[0:1, 4 * CH:5 * CH], in_=rows[0:1, 0:CH],
                func=AF.Sqrt, bias=csm[0:1, 1:2], scale=1.0),
                reads=[r_csm], rmw=[r_rows])
            p.op("ve", lambda e: e.reciprocal(
                out=rows[0:1, CH:2 * CH], in_=rows[0:1, 4 * CH:5 * CH]),
                rmw=[r_rows])
            p.op("te", lambda e: e.matmul(
                pb[7][:, 0:CH], ones[0:1, 0:128], rows[0:1, CH:2 * CH],
                start=True, stop=True),
                reads=[r_rows, r_ones], writes=[pbres[7]])
            for f in range(ZCC):
                p.op("ve", lambda e, f=f: e.scalar_tensor_tensor(
                    out=xbc[:, f * CH:(f + 1) * CH],
                    in0=xbc[:, f * CH:(f + 1) * CH],
                    scalar=normw[:, f:f + 1],
                    in1=pb[7][:, 0:CH], op0=ALU.mult, op1=ALU.mult),
                    reads=[pbres[7], r_normw], rmw=[r_xbcx])
            for m in range(FD):
                bank = m % 2
                for half in range(2):
                    wb = (m * 2 + half) % 2
                    load_w_slice(wb * FD * 128,
                                 woutr[:, half * 16:(half + 1) * 16,
                                       m * 128:(m + 1) * 128],
                                 r_wbuf[wb], 16, 128, wbuf, "w%d" % wb)
                    mm = []
                    for kk in range(16):
                        k = half * 16 + kk
                        mm.append(lambda e, k=k, kk=kk, wb=wb, bank=bank:
                                  e.matmul(
                            pb[bank][:, 0:CH],
                            wbuf[:, wb * FD * 128 + kk * 128:
                                 wb * FD * 128 + (kk + 1) * 128],
                            xbc[:, k * CH:(k + 1) * CH],
                            start=(k == 0), stop=(k == 31)))
                    p.op("te", mm, reads=[r_wbuf[wb], r_xbcx],
                         writes=[pbres[bank]] if half == 0 else (),
                         rmw=[pbres[bank]] if half else ())
                p.op("ve", lambda e, m=m, bank=bank: e.tensor_tensor(
                    out=u_c[:, m * CH:(m + 1) * CH],
                    in0=pb[bank][:, 0:CH],
                    in1=u_c[:, m * CH:(m + 1) * CH], op=ALU.add),
                    reads=[pbres[bank]], rmw=[r_uc])
                p.op("gp", lambda e, m=m, ci=ci: e.dma_start(
                    out=d_out[m * 128:(m + 1) * 128,
                              (ci - OWN0) * CH:(ci - OWN0 + 1) * CH],
                    in_=u_c[:, m * CH:(m + 1) * CH]),
                    reads=[r_uc], dma="st")

        def emit_states(ci):
            for h in range(H):
                for k in range(2):
                    p.op("ve", lambda e, h=h, k=k: e.tensor_scalar(
                        out=Bs[:, k * 128:(k + 1) * 128],
                        in0=BT[:, k * 128:(k + 1) * 128],
                        scalar1=tts[:, 256 + k * 64 + h:256 + k * 64 + h + 1],
                        scalar2=None, op0=ALU.mult),
                        reads=[r_BT, r_dtdsT], writes=[r_Bs[k]])
                bank = h // 8
                col = (h % 8) * 64
                mm = [lambda e, h=h, k=k, bank=bank, col=col: e.matmul(
                    pb[bank][:, col:col + 64],
                    Bs[:, k * 128:(k + 1) * 128],
                    rawXT[:, k * DIN + h * HD:k * DIN + (h + 1) * HD],
                    start=(k == 0), stop=(k == 1)) for k in range(2)]
                p.op("te", mm, reads=[r_Bs[0], r_Bs[1], r_XT],
                     writes=[pbres[bank]] if h % 8 == 0 else (),
                     rmw=[pbres[bank]] if h % 8 else ())

        def emit_ebT(ci):
            p.op("te", lambda e: e.transpose(
                pb[5][0:1, 0:64], ecol[:, 1:2], id128[0:64, 0:64]),
                reads=[r_ecol, r_id], writes=[pbres[5]])
            p.op("sc", lambda e: e.copy(out=erow[0:1, :],
                                        in_=pb[5][0:1, 0:64]),
                 reads=[pbres[5]], writes=[r_erow])
            p.op("te", lambda e: e.matmul(
                pb[5][:, 0:64], ones[0:1, 0:128], erow[0:1, :],
                start=True, stop=True),
                reads=[r_erow, r_ones], writes=[pbres[5]])
            p.op("sc", lambda e: e.copy(out=ebT[:, :], in_=pb[5][:, 0:64]),
                 reads=[pbres[5]], writes=[r_ebT])

        def emit_supdate(ci):
            for b in range(8):
                p.op("ve", lambda e, b=b: e.tensor_tensor(
                    out=S[:, b * 512:(b + 1) * 512].rearrange(
                        "p (h x) -> p h x", h=8),
                    in0=S[:, b * 512:(b + 1) * 512].rearrange(
                        "p (h x) -> p h x", h=8),
                    in1=ebT[:, b * 8:(b + 1) * 8].rearrange(
                        "p (h o) -> p h o", o=1).to_broadcast([128, 8, 64]),
                    op=ALU.mult),
                    reads=[r_ebT], rmw=[r_S])
                p.op("ve", lambda e, b=b, ci=ci: e.scalar_tensor_tensor(
                    out=S[:, b * 512:(b + 1) * 512],
                    in0=pb[b][:, 0:512], scalar=valid[:, ci:ci + 1],
                    in1=S[:, b * 512:(b + 1) * 512],
                    op0=ALU.mult, op1=ALU.add),
                    reads=[pbres[b], r_valid], rmw=[r_S])
            if TAPS and ci >= OWN0:
                p.op("gp", [lambda e, ci=ci: e.dma_start(
                    out=d_taps["S%d" % ci], in_=S[:, :])],
                    reads=[r_S], dma="tap")
                p.op("gp", [lambda e, ci=ci: e.dma_start(
                    out=d_taps["e%d" % ci], in_=ebT[:, :])],
                    reads=[r_ebT], dma="tap")

        for ci in range(NCH):
            own = ci >= OWN0
            emit_ln(ci, lambda f: xn_c[:, f * CH:(f + 1) * CH], r_xnc,
                    u_c, r_uc)
            mlist = [(4096 + 128 * cc, 128, "raw", cc)
                     for cc in range(34)]
            mlist.append((8448, 64, "dt", 0))
            emit_inproj(ci, mlist, xn_c, r_xnc)
            emit_conv(ci, 34)
            emit_dt(ci, own)
            emit_transposes(ci, own)
            if own:
                emit_G2(ci)
                emit_inproj(ci, [(128 * zc, 128, "z", zc)
                                 for zc in range(ZCC)], xn_c, r_xnc)
                emit_Y(ci)
                emit_rms_outproj(ci)
            if ci < NCH - 1:
                emit_ebT(ci)
                emit_states(ci)
                emit_supdate(ci)

        p.fence()
        o4 = 0

        def nxt4(cols):
            nonlocal o4
            a = o4
            o4 += cols
            return a
        O_XN4 = nxt4(FD * TOK)
        O_ACC = nxt4(FD * TOK)
        O_HB = nxt4(JCC * TOK)
        O_SG = nxt4(TOK)
        O_CEB = nxt4(TOK)
        O_CET = nxt4(TOK)
        O_RT = nxt4(640)
        O_CEST = nxt4(TOK)
        O_WB4 = nxt4(2 * FD * 128)
        assert o4 <= AR_COLS, o4

        xn = seg(O_XN4, FD * TOK)
        acc = seg(O_ACC, FD * TOK)
        hbuf = seg(O_HB, JCC * TOK)
        u4 = seg(O_HB, FD * CH)
        sg = seg(O_SG, TOK)
        ceB = seg(O_CEB, TOK)
        ceT = seg(O_CET, TOK)
        rt = seg(O_RT, 640)
        wb4 = seg(O_WB4, 2 * FD * 128)

        r_xn, r_acc, r_hb = Res(), Res(), Res()
        r_sg, r_ceB, r_ceT, r_cest = Res(), Res(), Res(), Res()
        r_ce = Res()
        r_probs, r_work, r_eq, r_amask = Res(), Res(), Res(), Res()
        r_m8, r_ren, r_cum, r_act, r_rw, r_col = (Res() for _ in range(6))
        r_wb4 = [Res(), Res()]

        CEA = 0
        PRB = 256
        WRK = PRB + 32
        EQ = WRK + 32
        AMK = EQ + 32
        M8 = AMK + 32
        REN = M8 + 8
        CUM = REN + 8
        ACT = CUM + 8
        RW = ACT + 8
        COLS = RW + 8

        for ci in range(OWN0, NCH):
            emit_ln(ci, lambda f, ci=ci: xn[:, f * TOK + (ci - OWN0) * CH:
                                            f * TOK + (ci - OWN0 + 1) * CH],
                    r_xn, u4, r_hb, usem="u4")

        for tc in range(8):
            mm = [lambda e, f=f, tc=tc: e.matmul(
                pb[4][:, 0:32],
                xn[:, f * TOK + tc * 128:f * TOK + (tc + 1) * 128],
                gwT[:, f * 32:(f + 1) * 32],
                start=(f == 0), stop=(f == FD - 1)) for f in range(FD)]
            p.op("te", mm, reads=[r_xn, r_gwT], writes=[pbres[4]])
            p.op("ve", lambda e: e.tensor_copy(out=rt[:, PRB:PRB + 32],
                                               in_=pb[4][:, 0:32]),
                 reads=[pbres[4]], writes=[r_probs])
            p.op("ve", lambda e: e.reduce_max(
                out=rt[:, COLS:COLS + 1], in_=rt[:, PRB:PRB + 32], axis=AX.X),
                reads=[r_probs], writes=[r_col])
            p.op("ve", lambda e: e.tensor_scalar(
                out=rt[:, PRB:PRB + 32], in0=rt[:, PRB:PRB + 32],
                scalar1=rt[:, COLS:COLS + 1], scalar2=None, op0=ALU.subtract),
                reads=[r_col], rmw=[r_probs])
            p.op("sc", lambda e: e.activation(
                out=rt[:, WRK:WRK + 32], in_=rt[:, PRB:PRB + 32],
                func=AF.Exp), reads=[r_probs], writes=[r_work])
            p.op("ve", lambda e: e.reduce_sum(
                out=rt[:, COLS:COLS + 1], in_=rt[:, WRK:WRK + 32], axis=AX.X),
                reads=[r_work], rmw=[r_col])
            p.op("ve", lambda e: e.reciprocal(
                out=rt[:, COLS + 1:COLS + 2], in_=rt[:, COLS:COLS + 1]),
                rmw=[r_col])
            p.op("ve", lambda e: e.tensor_scalar(
                out=rt[:, PRB:PRB + 32], in0=rt[:, WRK:WRK + 32],
                scalar1=rt[:, COLS + 1:COLS + 2], scalar2=None, op0=ALU.mult),
                reads=[r_work, r_col], rmw=[r_probs])
            p.op("ve", lambda e: e.max(out=rt[:, M8:M8 + 8],
                                       in_=rt[:, PRB:PRB + 32]),
                 reads=[r_probs], writes=[r_m8])
            p.op("ve", lambda e: e.reduce_sum(
                out=rt[:, COLS:COLS + 1], in_=rt[:, M8:M8 + 8], axis=AX.X),
                reads=[r_m8], rmw=[r_col])
            p.op("ve", lambda e: e.tensor_scalar(
                out=rt[:, COLS:COLS + 1], in0=rt[:, COLS:COLS + 1],
                scalar1=1e-12, scalar2=None, op0=ALU.max), rmw=[r_col])
            p.op("ve", lambda e: e.reciprocal(
                out=rt[:, COLS + 1:COLS + 2], in_=rt[:, COLS:COLS + 1]),
                rmw=[r_col])
            p.op("ve", lambda e: e.tensor_scalar(
                out=rt[:, REN:REN + 8], in0=rt[:, M8:M8 + 8],
                scalar1=rt[:, COLS + 1:COLS + 2], scalar2=None, op0=ALU.mult),
                reads=[r_m8, r_col], writes=[r_ren])
            p.op("ve", lambda e: e.tensor_tensor_scan(
                out=rt[:, CUM:CUM + 8], data0=ones[:, 0:8],
                data1=rt[:, REN:REN + 8], initial=0.0,
                op0=ALU.mult, op1=ALU.add),
                reads=[r_ren, r_ones], writes=[r_cum])
            p.op("ve", lambda e: e.memset(rt[:, ACT:ACT + 1], 0.0),
                 writes=[r_act])
            p.op("ve", lambda e: e.tensor_copy(
                out=rt[:, ACT + 1:ACT + 8], in_=rt[:, CUM:CUM + 7]),
                reads=[r_cum], rmw=[r_act])
            p.op("ve", lambda e: e.tensor_scalar(
                out=rt[:, ACT:ACT + 8], in0=rt[:, ACT:ACT + 8],
                scalar1=0.8, scalar2=None, op0=ALU.is_lt), rmw=[r_act])
            p.op("ve", lambda e: e.tensor_tensor(
                out=rt[:, RW:RW + 8], in0=rt[:, M8:M8 + 8],
                in1=rt[:, ACT:ACT + 8], op=ALU.mult),
                reads=[r_m8, r_act], writes=[r_rw])
            p.op("ve", lambda e: e.reduce_sum(
                out=rt[:, COLS:COLS + 1], in_=rt[:, RW:RW + 8], axis=AX.X),
                reads=[r_rw], rmw=[r_col])
            p.op("ve", lambda e: e.tensor_scalar(
                out=rt[:, COLS:COLS + 1], in0=rt[:, COLS:COLS + 1],
                scalar1=1e-12, scalar2=None, op0=ALU.max), rmw=[r_col])
            p.op("ve", lambda e: e.reciprocal(
                out=rt[:, COLS + 2:COLS + 3], in_=rt[:, COLS:COLS + 1]),
                rmw=[r_col])
            p.op("ve", lambda e: e.memset(rt[:, AMK:AMK + 32], 0.0),
                 writes=[r_amask])
            p.op("ve", lambda e: e.tensor_copy(out=rt[:, WRK:WRK + 32],
                                               in_=rt[:, PRB:PRB + 32]),
                 reads=[r_probs], writes=[r_work])
            for k in range(8):
                p.op("ve", lambda e, k=k: e.tensor_scalar(
                    out=rt[:, EQ:EQ + 32], in0=rt[:, WRK:WRK + 32],
                    scalar1=rt[:, M8 + k:M8 + k + 1], scalar2=None,
                    op0=ALU.is_equal),
                    reads=[r_work, r_m8], writes=[r_eq])
                p.op("ve", lambda e: e.tensor_tensor(
                    out=rt[:, EQ:EQ + 32], in0=rt[:, EQ:EQ + 32],
                    in1=iotam[:, :], op=ALU.mult),
                    reads=[r_iotam], rmw=[r_eq])
                p.op("ve", lambda e: e.tensor_reduce(
                    out=rt[:, COLS + 3:COLS + 4], in_=rt[:, EQ:EQ + 32],
                    axis=AX.X, op=ALU.min), reads=[r_eq], rmw=[r_col])
                p.op("ve", lambda e: e.tensor_scalar(
                    out=rt[:, EQ:EQ + 32], in0=iotam[:, :],
                    scalar1=rt[:, COLS + 3:COLS + 4], scalar2=None,
                    op0=ALU.is_equal),
                    reads=[r_iotam, r_col], rmw=[r_eq])
                p.op("ve", lambda e, k=k: e.scalar_tensor_tensor(
                    out=rt[:, AMK:AMK + 32], in0=rt[:, EQ:EQ + 32],
                    scalar=rt[:, ACT + k:ACT + k + 1],
                    in1=rt[:, AMK:AMK + 32], op0=ALU.mult, op1=ALU.add),
                    reads=[r_eq, r_act], rmw=[r_amask])
                p.op("ve", lambda e: e.scalar_tensor_tensor(
                    out=rt[:, WRK:WRK + 32], in0=rt[:, EQ:EQ + 32],
                    scalar=-2.0, in1=rt[:, WRK:WRK + 32],
                    op0=ALU.mult, op1=ALU.add),
                    reads=[r_eq], rmw=[r_work])
            p.op("ve", lambda e, tc=tc: e.scalar_tensor_tensor(
                out=rt[:, CEA + tc * 32:CEA + (tc + 1) * 32],
                in0=rt[:, PRB:PRB + 32], scalar=rt[:, COLS + 2:COLS + 3],
                in1=rt[:, AMK:AMK + 32], op0=ALU.mult, op1=ALU.mult),
                reads=[r_probs, r_col, r_amask],
                writes=[r_ce] if tc == 0 else (), rmw=[r_ce] if tc else ())
        for tc in range(8):
            bank = tc % 2
            p.op("te", lambda e, tc=tc, bank=bank: e.transpose(
                pb[bank][0:32, 0:128],
                rt[:, CEA + tc * 32:CEA + (tc + 1) * 32], id128[:, :]),
                reads=[r_ce, r_id], writes=[pbres[bank]])
            p.op("sc", lambda e, tc=tc, bank=bank: e.copy(
                out=ceT[0:32, tc * 128:(tc + 1) * 128],
                in_=pb[bank][0:32, 0:128]),
                reads=[pbres[bank]],
                writes=[r_ceT] if tc == 0 else (), rmw=[r_ceT] if tc else ())

        dmas = [lambda e, f=f: e.dma_start(
            out=acc[:, f * TOK:(f + 1) * TOK],
            in_=d_out[f * 128:(f + 1) * 128, :]) for f in range(FD)]
        p.op("sy", dmas, writes=[r_acc], dma="acc")

        cest = arena[0:1, O_CEST:O_CEST + TOK]

        for ex in range(NEXP):
            p.op("gp", lambda e, ex=ex: e.dma_start(
                out=cest, in_=ceT[ex:ex + 1, :]),
                reads=[r_ceT], writes=[r_cest], dma="ce")
            for th in range(2):
                p.op("te", lambda e, th=th: e.matmul(
                    pb[6][:, 0:512], ones[0:1, 0:128],
                    cest[:, th * 512:(th + 1) * 512], start=True, stop=True),
                    reads=[r_cest, r_ones], writes=[pbres[6]])
                p.op("ve", lambda e, th=th: e.tensor_copy(
                    out=ceB[:, th * 512:(th + 1) * 512], in_=pb[6][:, 0:512]),
                    reads=[pbres[6]],
                    writes=[r_ceB] if th == 0 else (),
                    rmw=[r_ceB] if th else ())
            for jc in range(JCC):
                for gu in range(2):
                    oc = jc + gu * 6
                    wb = oc % 2
                    p.op("sy", lambda e, ex=ex, oc=oc, wb=wb: e.dma_start(
                        out=wb4[:, wb * FD * 128:(wb + 1) * FD * 128]
                        .rearrange("p (k m) -> p k m", k=FD),
                        in_=d_wgu[ex:ex + 1, :, oc * 128:(oc + 1) * 128]
                        .rearrange("o (k p) m -> p (o k) m", p=128)),
                        writes=[r_wb4[wb]], dma="x%d" % wb)
                    for th in range(2):
                        bank = 2 * gu + th
                        mm = [lambda e, k=k, th=th, wb=wb, bank=bank:
                              e.matmul(
                            pb[bank][:, 0:512],
                            wb4[:, wb * FD * 128 + k * 128:
                                wb * FD * 128 + (k + 1) * 128],
                            xn[:, k * TOK + th * 512:k * TOK + (th + 1) * 512],
                            start=(k == 0), stop=(k == FD - 1))
                            for k in range(FD)]
                        p.op("te", mm, reads=[r_wb4[wb], r_xn],
                             writes=[pbres[bank]])
                        if gu == 0:
                            p.op("sc", lambda e, th=th, bank=bank:
                                 e.activation(
                                out=sg[:, th * 512:(th + 1) * 512],
                                in_=pb[bank][:, 0:512], func=AF.Silu),
                                reads=[pbres[bank]],
                                writes=[r_sg] if th == 0 else (),
                                rmw=[r_sg] if th else ())
                        else:
                            p.op("ve", lambda e, jc=jc, th=th, bank=bank:
                                 e.tensor_tensor(
                                out=hbuf[:, jc * TOK + th * 512:
                                         jc * TOK + (th + 1) * 512],
                                in0=pb[bank][:, 0:512],
                                in1=sg[:, th * 512:(th + 1) * 512],
                                op=ALU.mult),
                                reads=[pbres[bank], r_sg],
                                writes=[r_hb] if (jc == 0 and th == 0) and ex == 0
                                else (),
                                rmw=[r_hb] if not ((jc == 0 and th == 0)
                                                   and ex == 0) else ())
                            p.op("ve", lambda e, jc=jc, th=th:
                                 e.tensor_tensor(
                                out=hbuf[:, jc * TOK + th * 512:
                                         jc * TOK + (th + 1) * 512],
                                in0=hbuf[:, jc * TOK + th * 512:
                                         jc * TOK + (th + 1) * 512],
                                in1=ceB[:, th * 512:(th + 1) * 512],
                                op=ALU.mult),
                                reads=[r_ceB], rmw=[r_hb])
            for oc in range(FD):
                wb = oc % 2
                p.op("sy", lambda e, ex=ex, oc=oc, wb=wb: e.dma_start(
                    out=wb4[:, wb * FD * 128:wb * FD * 128 + JCC * 128]
                    .rearrange("p (k m) -> p k m", k=JCC),
                    in_=d_wd[ex:ex + 1, :, oc * 128:(oc + 1) * 128]
                    .rearrange("o (k p) m -> p (o k) m", p=128)),
                    writes=[r_wb4[wb]], dma="x%d" % wb)
                for th in range(2):
                    bank = 4 + th
                    mm = [lambda e, k=k, th=th, wb=wb, bank=bank: e.matmul(
                        pb[bank][:, 0:512],
                        wb4[:, wb * FD * 128 + k * 128:
                            wb * FD * 128 + (k + 1) * 128],
                        hbuf[:, k * TOK + th * 512:k * TOK + (th + 1) * 512],
                        start=(k == 0), stop=(k == JCC - 1))
                        for k in range(JCC)]
                    p.op("te", mm, reads=[r_wb4[wb], r_hb],
                         writes=[pbres[bank]])
                    p.op("ve", lambda e, oc=oc, th=th, bank=bank:
                         e.tensor_tensor(
                        out=acc[:, oc * TOK + th * 512:
                                oc * TOK + (th + 1) * 512],
                        in0=pb[bank][:, 0:512],
                        in1=acc[:, oc * TOK + th * 512:
                                oc * TOK + (th + 1) * 512], op=ALU.add),
                        reads=[pbres[bank]], rmw=[r_acc])
        dmas = [lambda e, f=f: e.dma_start(
            out=d_out[f * 128:(f + 1) * 128, :],
            in_=acc[:, f * TOK:(f + 1) * TOK]) for f in range(FD)]
        p.op("gp", dmas, reads=[r_acc], dma="st")
        final = p.cnt["d_st"]
        p.raw("gp", lambda e, v=final: e.wait_ge(p.sem["d_st"], v))

        block = ctx.enter_context(nc.Block())
        p.flush(block)
    return nc


_NC_CACHE = [None]


def kernel(L0_attn_u, L0_attn_W_in, L0_attn_conv_w, L0_attn_conv_b,
           L0_attn_dt_bias, L0_attn_A_log, L0_attn_Dp, L0_attn_norm_w,
           L0_attn_W_out, L0_ffn_gate_weight, L0_ffn_gate_up_proj,
           L0_ffn_down_proj):
    u = np.asarray(L0_attn_u, dtype=np.float32)
    w_in = np.ascontiguousarray(np.asarray(L0_attn_W_in, np.float32))
    w_out = np.ascontiguousarray(np.asarray(L0_attn_W_out, np.float32))
    conv_w = np.asarray(L0_attn_conv_w, np.float32)
    conv_b = np.asarray(L0_attn_conv_b, np.float32)
    dtb = np.ascontiguousarray(np.asarray(L0_attn_dt_bias,
                                          np.float32).reshape(64, 1))
    alog = np.ascontiguousarray(np.asarray(L0_attn_A_log,
                                           np.float32).reshape(64, 1))
    dp = np.asarray(L0_attn_Dp, np.float32)
    normw = np.asarray(L0_attn_norm_w, np.float32)
    gw = np.asarray(L0_ffn_gate_weight, np.float32)
    wgu = np.asarray(L0_ffn_gate_up_proj, np.float32)
    wd = np.asarray(L0_ffn_down_proj, np.float32)

    convw_l = np.ascontiguousarray(
        conv_w.reshape(CONVCC, 128, 4).transpose(1, 0, 2).reshape(128, -1))
    convb_l = np.ascontiguousarray(conv_b.reshape(CONVCC, 128).T)
    dpf_l = np.ascontiguousarray(np.broadcast_to(dp[None, :], (128, H)))
    normw_l = np.ascontiguousarray(normw.reshape(ZCC, 128).T)
    gwT_l = np.ascontiguousarray(gw.T)
    wgu_l = np.ascontiguousarray(wgu.transpose(0, 2, 1))
    wd_l = np.ascontiguousarray(wd.transpose(0, 2, 1))

    in_maps = []
    for c in range(8):
        b, j = divmod(c, 4)
        npad = 3072 - 1024 * j
        uT = np.concatenate([
            np.zeros((DM, npad), np.float32),
            np.ascontiguousarray(u[b, :1024 * j, :].T),
            np.ascontiguousarray(u[b, 1024 * j:1024 * (j + 1), :].T)], axis=1)
        vmask = np.zeros((NCH,), np.float32)
        vmask[OWN0 - 4 * j:] = 1.0
        valid_l = np.ascontiguousarray(
            np.broadcast_to(vmask[None, :], (128, NCH)))
        inval_l = np.ascontiguousarray(1.0 - valid_l)
        in_maps.append({
            "uT": np.ascontiguousarray(uT),
            "w_in": w_in, "w_out": w_out,
            "convw": convw_l, "convb": convb_l,
            "dtb": dtb, "alog": alog, "dpf": dpf_l, "normw": normw_l,
            "gwT": gwT_l, "wgu": wgu_l, "wd": wd_l,
            "valid": valid_l, "invalid": inval_l,
        })

    if _NC_CACHE[0] is None:
        _NC_CACHE[0] = build_program()
    nc = _NC_CACHE[0]
    res = run_bass_kernel_spmd(nc, in_maps, core_ids=list(range(8)))
    out = np.empty((2, 4096, DM), np.float32)
    for c in range(8):
        b, j = divmod(c, 4)
        out[b, 1024 * j:1024 * (j + 1), :] = np.asarray(
            res.results[c]["out"]).T
    return out


# version 02
# speedup vs baseline: 1.0882x; 1.0882x over previous
import numpy as np
from contextlib import ExitStack

import concourse.bass as bass
import concourse.mybir as mybir
from concourse.bass_utils import run_bass_kernel_spmd

F32 = mybir.dt.float32
ALU = mybir.AluOpType
AF = mybir.ActivationFunctionType
AX = mybir.AxisListType

CH = 256
NCH = 16
OWN0 = 12
H = 64
HD = 64
DM = 2048
FD = DM // 128
DIN = 4096
ZCC = DIN // 128
CONVCC = 34
NEXP = 32
INTER = 768
JCC = INTER // 128
TOK = 1024

ENG_METHOD = {"te": "tensor", "ve": "vector", "sc": "scalar",
              "gp": "gpsimd", "sy": "sync"}

TAPS = False


class Res:
    __slots__ = ("w", "r")

    def __init__(self):
        self.w = None
        self.r = []


class Prog:
    def __init__(self, nc, ctx):
        self.nc = nc
        self.ctx = ctx
        self.q = {k: [] for k in ENG_METHOD}
        self.sem = {}
        self.cnt = {}
        for k in ENG_METHOD:
            self.sem[k] = ctx.enter_context(nc.semaphore("sem_" + k))
            self.cnt[k] = 0
        self.lastwait = {}
        self.fence_state = None
        self.fence_absorbed = set()

    def dsem(self, name):
        key = "d_" + name
        if key not in self.sem:
            self.sem[key] = self.ctx.enter_context(
                self.nc.semaphore("sem" + key))
            self.cnt[key] = 0
        return key

    def _wait(self, en, dep):
        sem, val = dep
        if sem.startswith("d_"):
            val = max(val, self.cnt[sem])

        key = (en, sem)
        if self.lastwait.get(key, 0) >= val:
            return
        self.lastwait[key] = val
        s = self.sem[sem]
        self.q[en].append(lambda e, s=s, v=val: e.wait_ge(s, v))

    def op(self, en, build, reads=(), writes=(), rmw=(), dma=False):
        if self.fence_state is not None and en not in self.fence_absorbed:
            self.fence_absorbed.add(en)
            for sem, val in self.fence_state.items():
                if val > 0:
                    self._wait(en, (sem, val))
        deps = []
        for r in reads:
            if r.w:
                deps.append(r.w)
        for r in tuple(writes) + tuple(rmw):
            if r.w:
                deps.append(r.w)
            deps.extend(r.r)
        byname = {}
        for s, v in deps:
            if byname.get(s, 0) < v:
                byname[s] = v
        for s, v in byname.items():
            self._wait(en, (s, v))
        builders = build if isinstance(build, (list, tuple)) else [build]
        semn = self.dsem(dma) if dma else en
        s = self.sem[semn]
        if dma:
            for b in builders:
                self.cnt[semn] += 16
                self.q[en].append(lambda e, b=b, s=s: b(e).then_inc(s, 16))
        else:
            for b in builders[:-1]:
                self.q[en].append(lambda e, b=b: b(e))
            self.cnt[semn] += 1
            self.q[en].append(
                lambda e, b=builders[-1], s=s: b(e).then_inc(s, 1))
        tok = (semn, self.cnt[semn])
        for r in reads:
            rr = [t for t in r.r if t[0] != tok[0]]
            rr.append(tok)
            r.r = rr
        for r in tuple(writes) + tuple(rmw):
            r.w = tok
            r.r = []
        return tok

    def fence(self):
        self.fence_state = dict(self.cnt)
        self.fence_absorbed = set()

    def raw(self, en, closure):
        self.q[en].append(closure)

    def flush(self, block):
        for en, meth in ENG_METHOD.items():
            lst = self.q[en]
            if not lst:
                continue

            def runner(e, lst=lst):
                for f in lst:
                    f(e)
            getattr(block, meth)(runner)
            self.q[en] = []


def build_program():
    nc = bass.Bass("TRN2", target_bir_lowering=False)

    d_uT = nc.dram_tensor("uT", [DM, NCH * CH], F32, kind="ExternalInput").ap()
    d_win = nc.dram_tensor("w_in", [DM, 8512], F32, kind="ExternalInput").ap()
    d_wout = nc.dram_tensor("w_out", [DIN, DM], F32, kind="ExternalInput").ap()
    d_convw = nc.dram_tensor("convw", [128, CONVCC * 4], F32,
                             kind="ExternalInput").ap()
    d_convb = nc.dram_tensor("convb", [128, CONVCC], F32,
                             kind="ExternalInput").ap()
    d_dtb = nc.dram_tensor("dtb", [64, 1], F32, kind="ExternalInput").ap()
    d_alog = nc.dram_tensor("alog", [64, 1], F32, kind="ExternalInput").ap()
    d_dpf = nc.dram_tensor("dpf", [128, H], F32, kind="ExternalInput").ap()
    d_normw = nc.dram_tensor("normw", [128, ZCC], F32,
                             kind="ExternalInput").ap()
    d_gwT = nc.dram_tensor("gwT", [DM, NEXP], F32, kind="ExternalInput").ap()
    d_wgu = nc.dram_tensor("wgu", [NEXP, DM, 1536], F32,
                           kind="ExternalInput").ap()
    d_wd = nc.dram_tensor("wd", [NEXP, INTER, DM], F32,
                          kind="ExternalInput").ap()
    d_valid = nc.dram_tensor("valid", [128, NCH], F32,
                             kind="ExternalInput").ap()
    d_inval = nc.dram_tensor("invalid", [128, NCH], F32,
                             kind="ExternalInput").ap()
    d_out = nc.dram_tensor("out", [DM, TOK], F32, kind="ExternalOutput").ap()
    d_taps = {}
    if TAPS:
        for ci in range(OWN0, NCH):
            d_taps["S%d" % ci] = nc.dram_tensor(
                "tapS%d" % ci, [128, DIN], F32, kind="ExternalOutput").ap()
            d_taps["e%d" % ci] = nc.dram_tensor(
                "tape%d" % ci, [128, 64], F32, kind="ExternalOutput").ap()

    winr = d_win.rearrange("(k p) m -> p k m", p=128)
    woutr = d_wout.rearrange("(k p) m -> p k m", p=128)
    gwTr = d_gwT.rearrange("(k p) m -> p k m", p=128)

    with ExitStack() as ctx:
        p = Prog(nc, ctx)

        def sb(name, cols, rows=128):
            t = ctx.enter_context(nc.sbuf_tensor(name, [rows, cols], F32))
            return t, Res()

        ones, r_ones = sb("ones", 256)
        id128, r_id = sb("id128", 128)
        pen, r_pen = sb("pen", 2 * CH)
        identD, r_identD = sb("identD", 2 * CH)
        iotam, r_iotam = sb("iotam", 32)
        csm, r_csm = sb("csm", 8)
        valid, r_valid = sb("validsb", NCH)
        inval, r_inval = sb("invalsb", NCH)
        dtb, r_dtb = sb("dtbsb", 1, rows=64)
        alog, r_alog = sb("alogsb", 1, rows=64)
        acol, r_acol = sb("acolsb", 1, rows=64)
        dpf, r_dpf = sb("dpfsb", H)
        normw, r_normw = sb("normwsb", ZCC)
        convw, r_convw = sb("convwsb", CONVCC * 4)
        convb, r_convb = sb("convbsb", CONVCC)
        gwT, r_gwT = sb("gwTsb", FD * NEXP)
        rows, r_rows = sb("rowsbuf", 1280)
        ysq, _r_ysq0 = sb("ysqbuf", 2 * CH)
        r_ysq = [_r_ysq0, Res()]
        ebT, r_ebT = sb("ebTsb", H)
        erow, r_erow = sb("erowsb", H)
        ecol, r_ecol = sb("ecolsb", 2, rows=64)

        pb = []
        pbres = []
        for i in range(8):
            t = ctx.enter_context(nc.psum_tensor("pb%d" % i, [128, 512], F32))
            pb.append(t)
            pbres.append(Res())

        AR_COLS = 47760
        arena = ctx.enter_context(nc.sbuf_tensor("arena", [128, AR_COLS],
                                                 F32))

        def seg(off, cols, rows=128):
            return arena[0:rows, off:off + cols]

        o = 0

        def nxt(cols):
            nonlocal o
            a = o
            o += cols
            return a
        O_UC = nxt(FD * CH)
        O_XNC = nxt(FD * CH)
        O_RAWXT = nxt(CONVCC * 259)
        O_XBC = nxt(CONVCC * CH)
        O_HALO = nxt(CONVCC * 3)
        O_DTS = nxt(6 * CH)
        O_TTS = nxt(3 * 128)
        O_BT = nxt(2 * 128)
        O_G2 = nxt(2 * CH)
        O_MH = nxt(2 * CH)
        O_M0 = nxt(2 * CH)
        O_TT = nxt(2 * CH)
        O_CX = nxt(2 * CH)
        O_ET1 = nxt(CH)
        O_CONVT = nxt(2 * CH)
        O_BS = nxt(2 * 128)
        O_WBUF = nxt(2 * FD * 128)
        O_S = nxt(DIN)
        O_X2 = nxt(2 * CH + 2 * CH + 2 * CH + 2 * CH + CH + 2 * 128)
        assert o <= AR_COLS, o

        u_c = seg(O_UC, FD * CH)
        xn_c = seg(O_XNC, FD * CH)
        rawXT = seg(O_RAWXT, CONVCC * 259)
        xbc = seg(O_XBC, CONVCC * CH)
        halob = seg(O_HALO, CONVCC * 3)
        dts = seg(O_DTS, 6 * CH, rows=64)
        tts = seg(O_TTS, 3 * 128)
        BT = seg(O_BT, 2 * 128)
        G2sb = seg(O_G2, 2 * CH)
        mh = seg(O_MH, 2 * CH)
        m0 = seg(O_M0, 2 * CH)
        ttmp = seg(O_TT, 2 * CH)
        cexp = seg(O_CX, 2 * CH)
        expT1 = seg(O_ET1, CH)
        convt = seg(O_CONVT, 2 * CH)
        Bs = seg(O_BS, 2 * 128)
        wbuf = seg(O_WBUF, 2 * FD * 128)
        S = seg(O_S, DIN)
        x2 = O_X2
        mh_b = [mh, seg(x2, 2 * CH)]
        m0_b = [m0, seg(x2 + 2 * CH, 2 * CH)]
        ttmp_b = [ttmp, seg(x2 + 4 * CH, 2 * CH)]
        cexp_b = [cexp, seg(x2 + 6 * CH, 2 * CH)]
        expT1_b = [expT1, seg(x2 + 8 * CH, CH)]
        Bs_b = [Bs, seg(x2 + 9 * CH, 2 * 128)]

        r_uc, r_xnc = Res(), Res()
        r_raw = [Res() for _ in range(CONVCC)]
        r_XT = Res()
        r_xbcx, r_xbcBC = Res(), Res()
        r_halob = Res()
        r_dtraw, r_dt, r_dA, r_acum, r_dtds = (Res() for _ in range(5))
        r_nAT, r_dtT, r_dtdsT = Res(), Res(), Res()
        r_BT, r_G2 = Res(), Res()
        r_mh = [Res(), Res()]
        r_expT1 = [Res(), Res()]
        r_m0 = [[Res(), Res()], [Res(), Res()]]
        r_ttmp = [[Res(), Res()], [Res(), Res()]]
        r_cexp = [Res(), Res()]
        r_convt = [Res(), Res()]
        r_Bs = [[Res(), Res()], [Res(), Res()]]
        r_wbuf = [Res(), Res()]
        r_S = Res()

        for dst, src, res in [(valid, d_valid, r_valid),
                              (inval, d_inval, r_inval),
                              (dtb, d_dtb, r_dtb), (alog, d_alog, r_alog),
                              (dpf, d_dpf, r_dpf), (normw, d_normw, r_normw),
                              (convw, d_convw, r_convw),
                              (convb, d_convb, r_convb)]:
            p.op("sy", lambda e, dst=dst, src=src: e.dma_start(
                out=dst[:, :], in_=src), writes=[res], dma="cst")
        p.op("sy", lambda e: e.dma_start(
            out=gwT[:, :].rearrange("p (k m) -> p k m", k=FD), in_=gwTr),
            writes=[r_gwT], dma="cst")

        p.op("ve", lambda e: e.memset(ones[:, :], 1.0), writes=[r_ones])
        p.op("ve", lambda e: e.memset(csm[:, 0:1], 1.0), writes=[r_csm])
        p.op("ve", lambda e: e.memset(csm[:, 1:2], 1e-5), rmw=[r_csm])
        p.op("gp", lambda e: e.iota(id128[:, :], pattern=[[1, 128]], base=0,
                                    channel_multiplier=-1,
                                    allow_small_or_imprecise_dtypes=True),
             writes=[r_id])
        p.op("gp", lambda e: e.tensor_scalar(out=id128[:, :], in0=id128[:, :],
                                             scalar1=0.0, scalar2=None,
                                             op0=ALU.is_equal), rmw=[r_id])
        for k in range(2):
            p.op("gp", lambda e, k=k: e.iota(
                pen[:, k * CH:(k + 1) * CH], pattern=[[1, CH]], base=-128 * k,
                channel_multiplier=-1, allow_small_or_imprecise_dtypes=True),
                rmw=[r_pen])
            p.op("gp", lambda e, k=k: e.tensor_scalar(
                out=identD[:, k * CH:(k + 1) * CH],
                in0=pen[:, k * CH:(k + 1) * CH],
                scalar1=0.0, scalar2=None, op0=ALU.is_equal),
                reads=[r_pen], rmw=[r_identD])
            p.op("gp", lambda e, k=k: e.tensor_scalar(
                out=pen[:, k * CH:(k + 1) * CH],
                in0=pen[:, k * CH:(k + 1) * CH],
                scalar1=0.0, scalar2=-1e30, op0=ALU.is_lt, op1=ALU.mult),
                rmw=[r_pen])
        p.op("gp", lambda e: e.iota(iotam[:, :], pattern=[[1, 32]], base=-999,
                                    channel_multiplier=0,
                                    allow_small_or_imprecise_dtypes=True),
             writes=[r_iotam])
        p.op("sc", lambda e: e.activation(out=acol[:, :], in_=alog[:, :],
                                          func=AF.Exp),
             reads=[r_alog], writes=[r_acol])
        p.op("ve", lambda e: e.tensor_scalar(out=acol[:, :], in0=acol[:, :],
                                             scalar1=-1.0, scalar2=None,
                                             op0=ALU.mult), rmw=[r_acol])
        p.op("ve", lambda e: e.memset(S[:, :], 0.0), writes=[r_S])
        p.op("ve", lambda e: e.memset(halob[:, :], 0.0), writes=[r_halob])

        def emit_ln(ci, xn_ap_fn, r_xn_dest, u_buf, r_ubuf, usem="u"):
            dmas = [lambda e, f=f, ci=ci: e.dma_start(
                out=u_buf[:, f * CH:(f + 1) * CH],
                in_=d_uT[f * 128:(f + 1) * 128, ci * CH:(ci + 1) * CH])
                for f in range(FD)]
            p.op("sy", dmas, writes=[r_ubuf], dma=usem)
            for f in range(FD):
                ys = f % 2
                p.op("sc", lambda e, f=f, ys=ys: e.square(
                    out=ysq[:, ys * CH:(ys + 1) * CH],
                    in_=u_buf[:, f * CH:(f + 1) * CH]),
                    reads=[r_ubuf], writes=[r_ysq[ys]])
                p.op("te", lambda e, f=f: e.matmul(
                    pb[6][0:1, 0:CH], ones[:, 0:1],
                    u_buf[:, f * CH:(f + 1) * CH],
                    start=(f == 0), stop=(f == FD - 1)),
                    reads=[r_ubuf, r_ones],
                    writes=[pbres[6]] if f == 0 else (),
                    rmw=[pbres[6]] if f else ())
                p.op("te", lambda e, f=f, ys=ys: e.matmul(
                    pb[7][0:1, 0:CH], ones[:, 0:1],
                    ysq[:, ys * CH:(ys + 1) * CH],
                    start=(f == 0), stop=(f == FD - 1)),
                    reads=[r_ysq[ys], r_ones],
                    writes=[pbres[7]] if f == 0 else (),
                    rmw=[pbres[7]] if f else ())
            p.op("sc", lambda e: e.copy(out=rows[0:1, 0:CH],
                                        in_=pb[6][0:1, 0:CH]),
                 reads=[pbres[6]], rmw=[r_rows])
            p.op("sc", lambda e: e.copy(out=rows[0:1, CH:2 * CH],
                                        in_=pb[7][0:1, 0:CH]),
                 reads=[pbres[7]], rmw=[r_rows])
            p.op("ve", lambda e: e.tensor_scalar(
                out=rows[0:1, 2 * CH:3 * CH], in0=rows[0:1, 0:CH],
                scalar1=1.0 / DM, scalar2=None, op0=ALU.mult), rmw=[r_rows])
            p.op("ve", lambda e: e.tensor_scalar(
                out=rows[0:1, CH:2 * CH], in0=rows[0:1, CH:2 * CH],
                scalar1=1.0 / DM, scalar2=None, op0=ALU.mult), rmw=[r_rows])
            p.op("ve", lambda e: e.tensor_tensor(
                out=rows[0:1, 0:CH], in0=rows[0:1, 2 * CH:3 * CH],
                in1=rows[0:1, 2 * CH:3 * CH], op=ALU.mult), rmw=[r_rows])
            p.op("ve", lambda e: e.tensor_tensor(
                out=rows[0:1, 0:CH], in0=rows[0:1, CH:2 * CH],
                in1=rows[0:1, 0:CH], op=ALU.subtract), rmw=[r_rows])
            p.op("sc", lambda e: e.activation(
                out=rows[0:1, 4 * CH:5 * CH], in_=rows[0:1, 0:CH],
                func=AF.Sqrt, bias=csm[0:1, 1:2], scale=1.0),
                reads=[r_csm], rmw=[r_rows])
            p.op("ve", lambda e: e.reciprocal(
                out=rows[0:1, 3 * CH:4 * CH], in_=rows[0:1, 4 * CH:5 * CH]),
                rmw=[r_rows])
            p.op("te", lambda e: e.matmul(
                pb[6][:, 0:CH], ones[0:1, 0:128], rows[0:1, 2 * CH:3 * CH],
                start=True, stop=True),
                reads=[r_rows, r_ones], writes=[pbres[6]])
            p.op("te", lambda e: e.matmul(
                pb[7][:, 0:CH], ones[0:1, 0:128], rows[0:1, 3 * CH:4 * CH],
                start=True, stop=True),
                reads=[r_rows, r_ones], writes=[pbres[7]])
            for f in range(FD):
                ap = xn_ap_fn(f)
                p.op("ve", lambda e, f=f, ap=ap: e.scalar_tensor_tensor(
                    out=ap, in0=pb[6][:, 0:CH], scalar=-1.0,
                    in1=u_buf[:, f * CH:(f + 1) * CH],
                    op0=ALU.mult, op1=ALU.add),
                    reads=[pbres[6], r_ubuf],
                    writes=[r_xn_dest] if f == 0 else (),
                    rmw=[r_xn_dest] if f else ())
                p.op("ve", lambda e, ap=ap: e.tensor_tensor(
                    out=ap, in0=ap, in1=pb[7][:, 0:CH], op=ALU.mult),
                    reads=[pbres[7]], rmw=[r_xn_dest])

        def load_w_slice(dst_off, src_ap, res, nk, width, buf, sem):
            p.op("sy", lambda e: e.dma_start(
                out=buf[:, dst_off:dst_off + nk * width].rearrange(
                    "p (k m) -> p k m", k=nk),
                in_=src_ap), writes=[res], dma=sem)

        def emit_inproj(ci, mlist, rhs_ap, r_rhs):
            for i, (c0, width, kind, idx) in enumerate(mlist):
                wb = i % 2
                bank = i % 2
                load_w_slice(wb * FD * 128, winr[:, :, c0:c0 + width],
                             r_wbuf[wb], FD, width, wbuf, "w%d" % wb)
                mm = [lambda e, k=k, wb=wb, bank=bank, width=width: e.matmul(
                    pb[bank][0:width, 0:CH],
                    wbuf[:, wb * FD * 128 + k * width:
                         wb * FD * 128 + (k + 1) * width],
                    rhs_ap[:, k * CH:(k + 1) * CH],
                    start=(k == 0), stop=(k == FD - 1)) for k in range(FD)]
                p.op("te", mm, reads=[r_wbuf[wb], r_rhs],
                     writes=[pbres[bank]])
                if kind == "raw":
                    cc = idx
                    p.op("gp", lambda e, cc=cc: e.tensor_copy(
                        out=rawXT[:, cc * 259:cc * 259 + 3],
                        in_=halob[:, cc * 3:cc * 3 + 3]),
                        reads=[r_halob], writes=[r_raw[cc]], rmw=[r_XT])
                    p.op("sc", lambda e, cc=cc, bank=bank: e.copy(
                        out=rawXT[:, cc * 259 + 3:cc * 259 + 259],
                        in_=pb[bank][:, 0:CH]),
                        reads=[pbres[bank]], rmw=[r_raw[cc], r_XT])
                elif kind == "dt":
                    p.op("sc", lambda e, bank=bank: e.copy(
                        out=dts[:, 0:CH], in_=pb[bank][0:64, 0:CH]),
                        reads=[pbres[bank]], writes=[r_dtraw])
                elif kind == "z":
                    zc = idx
                    p.op("sc", lambda e, zc=zc, bank=bank: e.activation(
                        out=xbc[:, zc * CH:(zc + 1) * CH],
                        in_=pb[bank][:, 0:CH], func=AF.Silu),
                        reads=[pbres[bank]],
                        writes=[r_xbcx] if zc == 0 else (),
                        rmw=[r_xbcx] if zc else ())

        def emit_conv(ci, ncc):
            for cc in [32, 33] + list(range(32)):
                ct = cc % 2
                p.op("ve", lambda e, cc=cc, ct=ct: e.tensor_scalar(
                    out=convt[:, ct * CH:(ct + 1) * CH],
                    in0=rawXT[:, cc * 259 + 0:cc * 259 + CH],
                    scalar1=convw[:, cc * 4:cc * 4 + 1], scalar2=None,
                    op0=ALU.mult),
                    reads=[r_raw[cc], r_convw], writes=[r_convt[ct]])
                for k in range(1, 4):
                    p.op("ve", lambda e, cc=cc, k=k, ct=ct:
                         e.scalar_tensor_tensor(
                        out=convt[:, ct * CH:(ct + 1) * CH],
                        in0=rawXT[:, cc * 259 + k:cc * 259 + k + CH],
                        scalar=convw[:, cc * 4 + k:cc * 4 + k + 1],
                        in1=convt[:, ct * CH:(ct + 1) * CH],
                        op0=ALU.mult, op1=ALU.add),
                        reads=[r_raw[cc], r_convw], rmw=[r_convt[ct]])
                tgt = (r_xbcx if cc < 32 else r_xbcBC)
                p.op("sc", lambda e, cc=cc, ct=ct: e.activation(
                    out=xbc[:, cc * CH:(cc + 1) * CH],
                    in_=convt[:, ct * CH:(ct + 1) * CH], func=AF.Silu,
                    bias=convb[:, cc:cc + 1], scale=1.0),
                    reads=[r_convt[ct], r_convb],
                    writes=[tgt] if cc in (0, 32) else (),
                    rmw=[tgt] if cc not in (0, 32) else ())
            if ci < NCH - 1:
                for cc in [32, 33] + list(range(32)):
                    p.op("gp", lambda e, cc=cc: e.tensor_copy(
                        out=halob[:, cc * 3:cc * 3 + 3],
                        in_=rawXT[:, cc * 259 + CH:cc * 259 + CH + 3]),
                        reads=[r_raw[cc]], rmw=[r_halob])

        def emit_dt(ci, own):
            p.op("sc", lambda e: e.activation(
                out=dts[:, CH:2 * CH], in_=dts[:, 0:CH], func=AF.Exp,
                bias=dtb[:, 0:1], scale=1.0),
                reads=[r_dtraw, r_dtb], writes=[r_dt])
            p.op("sc", lambda e: e.activation(
                out=dts[:, 5 * CH:6 * CH], in_=dts[:, CH:2 * CH], func=AF.Ln,
                bias=csm[0:64, 0:1], scale=1.0),
                reads=[r_csm], rmw=[r_dt])
            p.op("ve", lambda e: e.tensor_scalar(
                out=dts[:, 2 * CH:3 * CH], in0=dts[:, 5 * CH:6 * CH],
                scalar1=acol[:, 0:1], scalar2=None, op0=ALU.mult),
                reads=[r_dt, r_acol], writes=[r_dA])
            p.op("ve", lambda e: e.tensor_tensor_scan(
                out=dts[:, 3 * CH:4 * CH], data0=ones[0:64, 0:CH],
                data1=dts[:, 2 * CH:3 * CH], initial=0.0,
                op0=ALU.mult, op1=ALU.add),
                reads=[r_dA, r_ones], writes=[r_acum])
            p.op("sc", lambda e: e.activation(
                out=ecol[:, 0:1], in_=dts[:, 4 * CH - 1:4 * CH],
                func=AF.Exp), reads=[r_acum], writes=[r_ecol])
            p.op("ve", lambda e, ci=ci: e.scalar_tensor_tensor(
                out=ecol[:, 1:2], in0=ecol[:, 0:1],
                scalar=valid[0:64, ci:ci + 1], in1=inval[0:64, ci:ci + 1],
                op0=ALU.mult, op1=ALU.add),
                reads=[r_valid, r_inval], rmw=[r_ecol])
            p.op("sc", lambda e: e.activation(
                out=dts[:, 4 * CH:5 * CH], in_=dts[:, 3 * CH:4 * CH],
                func=AF.Exp, bias=dts[:, 4 * CH - 1:4 * CH], scale=-1.0),
                reads=[r_acum], writes=[r_dtds])
            p.op("ve", lambda e: e.tensor_tensor(
                out=dts[:, 4 * CH:5 * CH], in0=dts[:, 4 * CH:5 * CH],
                in1=dts[:, 5 * CH:6 * CH], op=ALU.mult),
                reads=[r_dt], rmw=[r_dtds])

        def emit_transposes(ci, own):
            for k in range(2):
                bank = 2 + k
                p.op("te", lambda e, k=k, bank=bank: e.transpose(
                    pb[bank][:, 0:128],
                    xbc[:, 32 * CH + k * 128:32 * CH + (k + 1) * 128],
                    id128[:, :]),
                    reads=[r_xbcBC, r_id], writes=[pbres[bank]])
                p.op("sc", lambda e, k=k, bank=bank: e.copy(
                    out=BT[:, k * 128:(k + 1) * 128], in_=pb[bank][:, 0:128]),
                    reads=[pbres[bank]],
                    writes=[r_BT] if k == 0 else (), rmw=[r_BT] if k else ())
            specs = [(4, r_dtds, 256, 1.0, r_dtdsT)]
            if own:
                specs.append((3, r_acum, 0, -1.0, r_nAT))
                specs.append((5, r_dt, 128, 1.0, r_dtT))
            for (so, rsrc, do, scl, rdst) in specs:
                for k in range(2):
                    bank = 2 + k
                    p.op("te", lambda e, so=so, k=k, bank=bank: e.transpose(
                        pb[bank][:, 0:64],
                        dts[:, so * CH + k * 128:so * CH + (k + 1) * 128],
                        id128[0:64, 0:64]),
                        reads=[rsrc, r_id], writes=[pbres[bank]])
                    if scl == 1.0:
                        p.op("sc", lambda e, do=do, k=k, bank=bank: e.copy(
                            out=tts[:, do + k * 64:do + (k + 1) * 64],
                            in_=pb[bank][:, 0:64]),
                            reads=[pbres[bank]],
                            writes=[rdst] if k == 0 else (),
                            rmw=[rdst] if k else ())
                    else:
                        p.op("ve", lambda e, do=do, k=k, bank=bank:
                             e.tensor_scalar(
                            out=tts[:, do + k * 64:do + (k + 1) * 64],
                            in0=pb[bank][:, 0:64], scalar1=-1.0, scalar2=None,
                            op0=ALU.mult),
                            reads=[pbres[bank]],
                            writes=[rdst] if k == 0 else (),
                            rmw=[rdst] if k else ())
            first = [True]
            for f in range(ZCC):
                for k in range(2):
                    bank = 2 + ((f * 2 + k) % 2)
                    p.op("te", lambda e, f=f, k=k, bank=bank: e.transpose(
                        pb[bank][:, 0:128],
                        xbc[:, f * CH + k * 128:f * CH + (k + 1) * 128],
                        id128[:, :]),
                        reads=[r_xbcx, r_id], writes=[pbres[bank]])
                    if first[0]:
                        first[0] = False
                        wl, rl = [r_XT] + r_raw, ()
                    else:
                        wl, rl = (), [r_XT]
                    p.op("sc", lambda e, f=f, k=k, bank=bank: e.copy(
                        out=rawXT[:, k * DIN + f * 128:k * DIN + (f + 1) * 128],
                        in_=pb[bank][:, 0:128]),
                        reads=[pbres[bank]], writes=wl, rmw=rl)

        def emit_G2(ci):
            mm = [lambda e, k=k: e.matmul(
                pb[4][:, k * CH:(k + 1) * CH],
                xbc[:, 32 * CH + k * 128:32 * CH + (k + 1) * 128],
                xbc[:, 33 * CH:34 * CH], start=True, stop=True)
                for k in range(2)]
            p.op("te", mm, reads=[r_xbcBC], writes=[pbres[4]])
            for k in range(2):
                p.op("ve", lambda e, k=k: e.tensor_copy(
                    out=G2sb[:, k * CH:(k + 1) * CH],
                    in_=pb[4][:, k * CH:(k + 1) * CH]),
                    reads=[pbres[4]],
                    writes=[r_G2] if k == 0 else (), rmw=[r_G2] if k else ())

        def emit_Y(ci):
            def t1(h):
                bank = 4 + (h % 2)
                p.op("te", lambda e, h=h, bank=bank: e.matmul(
                    pb[bank][:, 0:CH],
                    id128[0:64, h:h + 1].to_broadcast([64, 128]),
                    dts[:, 3 * CH:4 * CH], start=True, stop=True),
                    reads=[r_acum, r_id], writes=[pbres[bank]])
            t1(0)
            for h in range(H):
                bank = 4 + (h % 2)
                bb = h % 2
                p.op("sc", lambda e, bb=bb, bank=bank: e.activation(
                    out=expT1_b[bb][:, :], in_=pb[bank][:, 0:CH],
                    func=AF.Exp),
                    reads=[pbres[bank]], writes=[r_expT1[bb]])
                p.op("gp", lambda e, bb=bb: e.tensor_tensor(
                    out=cexp_b[bb][:, 0:CH],
                    in0=xbc[:, 33 * CH:34 * CH], in1=expT1_b[bb][:, :],
                    op=ALU.mult),
                    reads=[r_xbcBC, r_expT1[bb]], writes=[r_cexp[bb]])
                for k in range(2):
                    p.op("ve", lambda e, k=k, bb=bb, bank=bank:
                         e.tensor_tensor(
                        out=ttmp_b[bb][:, k * CH:(k + 1) * CH],
                        in0=pb[bank][:, 0:CH],
                        in1=pen[:, k * CH:(k + 1) * CH], op=ALU.add),
                        reads=[pbres[bank], r_pen], writes=[r_ttmp[bb][k]])
                    p.op("sc", lambda e, k=k, h=h, bb=bb: e.activation(
                        out=m0_b[bb][:, k * CH:(k + 1) * CH],
                        in_=ttmp_b[bb][:, k * CH:(k + 1) * CH], func=AF.Exp,
                        bias=tts[:, k * 64 + h:k * 64 + h + 1], scale=1.0),
                        reads=[r_ttmp[bb][k], r_nAT], writes=[r_m0[bb][k]])
                    p.op("ve", lambda e, k=k, h=h, bb=bb:
                         e.scalar_tensor_tensor(
                        out=mh_b[bb][:, k * CH:(k + 1) * CH],
                        in0=G2sb[:, k * CH:(k + 1) * CH],
                        scalar=tts[:, 128 + k * 64 + h:128 + k * 64 + h + 1],
                        in1=m0_b[bb][:, k * CH:(k + 1) * CH],
                        op0=ALU.mult, op1=ALU.mult),
                        reads=[r_G2, r_dtT, r_m0[bb][k]],
                        writes=[r_mh[bb]] if k == 0 else (),
                        rmw=[r_mh[bb]] if k else ())
                    p.op("ve", lambda e, k=k, h=h, bb=bb:
                         e.scalar_tensor_tensor(
                        out=mh_b[bb][:, k * CH:(k + 1) * CH],
                        in0=identD[:, k * CH:(k + 1) * CH],
                        scalar=dpf[:, h:h + 1],
                        in1=mh_b[bb][:, k * CH:(k + 1) * CH],
                        op0=ALU.mult, op1=ALU.add),
                        reads=[r_identD, r_dpf], rmw=[r_mh[bb]])
                if h + 1 < H:
                    t1(h + 1)
                pairbank = 2 + ((h // 2) % 2)
                rb = (h % 2) * 64
                mm = [lambda e, h=h, k=k, pairbank=pairbank, rb=rb, bb=bb:
                      e.matmul(
                    pb[pairbank][rb:rb + 64, 0:CH],
                    rawXT[:, k * DIN + h * HD:k * DIN + (h + 1) * HD],
                    mh_b[bb][:, k * CH:(k + 1) * CH],
                    start=(k == 0), stop=False) for k in range(2)]
                mm.append(lambda e, h=h, pairbank=pairbank, rb=rb, bb=bb:
                          e.matmul(
                    pb[pairbank][rb:rb + 64, 0:CH],
                    S[:, h * HD:(h + 1) * HD],
                    cexp_b[bb][:, 0:CH],
                    start=False, stop=True))
                p.op("te", mm, reads=[r_XT, r_mh[bb], r_S, r_cexp[bb]],
                     writes=[pbres[pairbank]] if h % 2 == 0 else (),
                     rmw=[pbres[pairbank]] if h % 2 else ())
                if h % 2 == 1:
                    pair = h // 2
                    ys = pair % 2
                    p.op("ve", lambda e, pair=pair, pairbank=pairbank:
                         e.tensor_tensor(
                        out=xbc[:, pair * CH:(pair + 1) * CH],
                        in0=pb[pairbank][:, 0:CH],
                        in1=xbc[:, pair * CH:(pair + 1) * CH], op=ALU.mult),
                        reads=[pbres[pairbank]], rmw=[r_xbcx])
                    p.op("sc", lambda e, pair=pair, ys=ys: e.square(
                        out=ysq[:, ys * CH:(ys + 1) * CH],
                        in_=xbc[:, pair * CH:(pair + 1) * CH]),
                        reads=[r_xbcx], writes=[r_ysq[ys]])
                    p.op("te", lambda e, pair=pair, ys=ys: e.matmul(
                        pb[6][0:1, 0:CH], ones[:, 0:1],
                        ysq[:, ys * CH:(ys + 1) * CH],
                        start=(pair == 0), stop=(pair == 31)),
                        reads=[r_ysq[ys], r_ones],
                        writes=[pbres[6]] if pair == 0 else (),
                        rmw=[pbres[6]] if pair else ())

        def emit_rms_outproj(ci):
            p.op("sc", lambda e: e.copy(out=rows[0:1, 0:CH],
                                        in_=pb[6][0:1, 0:CH]),
                 reads=[pbres[6]], rmw=[r_rows])
            p.op("ve", lambda e: e.tensor_scalar(
                out=rows[0:1, 0:CH], in0=rows[0:1, 0:CH],
                scalar1=1.0 / DIN, scalar2=None, op0=ALU.mult), rmw=[r_rows])
            p.op("sc", lambda e: e.activation(
                out=rows[0:1, 4 * CH:5 * CH], in_=rows[0:1, 0:CH],
                func=AF.Sqrt, bias=csm[0:1, 1:2], scale=1.0),
                reads=[r_csm], rmw=[r_rows])
            p.op("ve", lambda e: e.reciprocal(
                out=rows[0:1, CH:2 * CH], in_=rows[0:1, 4 * CH:5 * CH]),
                rmw=[r_rows])
            p.op("te", lambda e: e.matmul(
                pb[7][:, 0:CH], ones[0:1, 0:128], rows[0:1, CH:2 * CH],
                start=True, stop=True),
                reads=[r_rows, r_ones], writes=[pbres[7]])
            for f in range(ZCC):
                p.op("ve", lambda e, f=f: e.scalar_tensor_tensor(
                    out=xbc[:, f * CH:(f + 1) * CH],
                    in0=xbc[:, f * CH:(f + 1) * CH],
                    scalar=normw[:, f:f + 1],
                    in1=pb[7][:, 0:CH], op0=ALU.mult, op1=ALU.mult),
                    reads=[pbres[7], r_normw], rmw=[r_xbcx])
            for m in range(FD):
                bank = m % 2
                for half in range(2):
                    wb = (m * 2 + half) % 2
                    load_w_slice(wb * FD * 128,
                                 woutr[:, half * 16:(half + 1) * 16,
                                       m * 128:(m + 1) * 128],
                                 r_wbuf[wb], 16, 128, wbuf, "w%d" % wb)
                    mm = []
                    for kk in range(16):
                        k = half * 16 + kk
                        mm.append(lambda e, k=k, kk=kk, wb=wb, bank=bank:
                                  e.matmul(
                            pb[bank][:, 0:CH],
                            wbuf[:, wb * FD * 128 + kk * 128:
                                 wb * FD * 128 + (kk + 1) * 128],
                            xbc[:, k * CH:(k + 1) * CH],
                            start=(k == 0), stop=(k == 31)))
                    p.op("te", mm, reads=[r_wbuf[wb], r_xbcx],
                         writes=[pbres[bank]] if half == 0 else (),
                         rmw=[pbres[bank]] if half else ())
                p.op("ve", lambda e, m=m, bank=bank: e.tensor_tensor(
                    out=u_c[:, m * CH:(m + 1) * CH],
                    in0=pb[bank][:, 0:CH],
                    in1=u_c[:, m * CH:(m + 1) * CH], op=ALU.add),
                    reads=[pbres[bank]], rmw=[r_uc])
                p.op("gp", lambda e, m=m, ci=ci: e.dma_start(
                    out=d_out[m * 128:(m + 1) * 128,
                              (ci - OWN0) * CH:(ci - OWN0 + 1) * CH],
                    in_=u_c[:, m * CH:(m + 1) * CH]),
                    reads=[r_uc], dma="st")

        def emit_states(ci):
            for h in range(H):
                bb = h % 2
                for k in range(2):
                    p.op("ve", lambda e, h=h, k=k, bb=bb: e.tensor_scalar(
                        out=Bs_b[bb][:, k * 128:(k + 1) * 128],
                        in0=BT[:, k * 128:(k + 1) * 128],
                        scalar1=tts[:, 256 + k * 64 + h:256 + k * 64 + h + 1],
                        scalar2=None, op0=ALU.mult),
                        reads=[r_BT, r_dtdsT], writes=[r_Bs[bb][k]])
                bank = h // 8
                col = (h % 8) * 64
                mm = [lambda e, h=h, k=k, bank=bank, col=col, bb=bb: e.matmul(
                    pb[bank][:, col:col + 64],
                    Bs_b[bb][:, k * 128:(k + 1) * 128],
                    rawXT[:, k * DIN + h * HD:k * DIN + (h + 1) * HD],
                    start=(k == 0), stop=(k == 1)) for k in range(2)]
                p.op("te", mm, reads=[r_Bs[bb][0], r_Bs[bb][1], r_XT],
                     writes=[pbres[bank]] if h % 8 == 0 else (),
                     rmw=[pbres[bank]] if h % 8 else ())

        def emit_ebT(ci):
            p.op("te", lambda e: e.transpose(
                pb[5][0:1, 0:64], ecol[:, 1:2], id128[0:64, 0:64]),
                reads=[r_ecol, r_id], writes=[pbres[5]])
            p.op("sc", lambda e: e.copy(out=erow[0:1, :],
                                        in_=pb[5][0:1, 0:64]),
                 reads=[pbres[5]], writes=[r_erow])
            p.op("te", lambda e: e.matmul(
                pb[5][:, 0:64], ones[0:1, 0:128], erow[0:1, :],
                start=True, stop=True),
                reads=[r_erow, r_ones], writes=[pbres[5]])
            p.op("sc", lambda e: e.copy(out=ebT[:, :], in_=pb[5][:, 0:64]),
                 reads=[pbres[5]], writes=[r_ebT])

        def emit_supdate(ci):
            for b in range(8):
                p.op("gp", lambda e, b=b: e.tensor_tensor(
                    out=S[:, b * 512:(b + 1) * 512].rearrange(
                        "p (h x) -> p h x", h=8),
                    in0=S[:, b * 512:(b + 1) * 512].rearrange(
                        "p (h x) -> p h x", h=8),
                    in1=ebT[:, b * 8:(b + 1) * 8].rearrange(
                        "p (h o) -> p h o", o=1).to_broadcast([128, 8, 64]),
                    op=ALU.mult),
                    reads=[r_ebT], rmw=[r_S])
                p.op("ve", lambda e, b=b, ci=ci: e.scalar_tensor_tensor(
                    out=S[:, b * 512:(b + 1) * 512],
                    in0=pb[b][:, 0:512], scalar=valid[:, ci:ci + 1],
                    in1=S[:, b * 512:(b + 1) * 512],
                    op0=ALU.mult, op1=ALU.add),
                    reads=[pbres[b], r_valid], rmw=[r_S])
            if TAPS and ci >= OWN0:
                p.op("gp", [lambda e, ci=ci: e.dma_start(
                    out=d_taps["S%d" % ci], in_=S[:, :])],
                    reads=[r_S], dma="tap")
                p.op("gp", [lambda e, ci=ci: e.dma_start(
                    out=d_taps["e%d" % ci], in_=ebT[:, :])],
                    reads=[r_ebT], dma="tap")

        for ci in range(NCH):
            own = ci >= OWN0
            emit_ln(ci, lambda f: xn_c[:, f * CH:(f + 1) * CH], r_xnc,
                    u_c, r_uc)
            mlist = [(8448, 64, "dt", 0),
                     (4096 + 128 * 32, 128, "raw", 32),
                     (4096 + 128 * 33, 128, "raw", 33)]
            mlist += [(4096 + 128 * cc, 128, "raw", cc) for cc in range(32)]
            emit_inproj(ci, mlist, xn_c, r_xnc)
            emit_conv(ci, 34)
            emit_dt(ci, own)
            emit_transposes(ci, own)
            if own:
                emit_G2(ci)
                emit_inproj(ci, [(128 * zc, 128, "z", zc)
                                 for zc in range(ZCC)], xn_c, r_xnc)
                emit_Y(ci)
                emit_rms_outproj(ci)
            if ci < NCH - 1:
                emit_ebT(ci)
                emit_states(ci)
                emit_supdate(ci)

        p.fence()
        o4 = 0

        def nxt4(cols):
            nonlocal o4
            a = o4
            o4 += cols
            return a
        O_XN4 = nxt4(FD * TOK)
        O_ACC = nxt4(FD * TOK)
        O_HB = nxt4(JCC * TOK)
        O_SG = nxt4(TOK)
        O_CEB = nxt4(TOK)
        O_CET = nxt4(TOK)
        O_RT = nxt4(640)
        O_CEST = nxt4(TOK)
        O_WB4 = nxt4(2 * FD * 128)
        assert o4 <= AR_COLS, o4

        xn = seg(O_XN4, FD * TOK)
        acc = seg(O_ACC, FD * TOK)
        hbuf = seg(O_HB, JCC * TOK)
        u4 = seg(O_HB, FD * CH)
        sg = seg(O_SG, TOK)
        ceB = seg(O_CEB, TOK)
        ceT = seg(O_CET, TOK)
        rt = seg(O_RT, 640)
        wb4 = seg(O_WB4, 2 * FD * 128)

        r_xn, r_acc, r_hb = Res(), Res(), Res()
        r_sg, r_ceB, r_ceT, r_cest = Res(), Res(), Res(), Res()
        r_ce = Res()
        r_probs, r_work, r_eq, r_amask = Res(), Res(), Res(), Res()
        r_m8, r_ren, r_cum, r_act, r_rw, r_col = (Res() for _ in range(6))
        r_wb4 = [Res(), Res()]

        CEA = 0
        PRB = 256
        WRK = PRB + 32
        EQ = WRK + 32
        AMK = EQ + 32
        M8 = AMK + 32
        REN = M8 + 8
        CUM = REN + 8
        ACT = CUM + 8
        RW = ACT + 8
        COLS = RW + 8

        for ci in range(OWN0, NCH):
            emit_ln(ci, lambda f, ci=ci: xn[:, f * TOK + (ci - OWN0) * CH:
                                            f * TOK + (ci - OWN0 + 1) * CH],
                    r_xn, u4, r_hb, usem="u4")

        for tc in range(8):
            mm = [lambda e, f=f, tc=tc: e.matmul(
                pb[4][:, 0:32],
                xn[:, f * TOK + tc * 128:f * TOK + (tc + 1) * 128],
                gwT[:, f * 32:(f + 1) * 32],
                start=(f == 0), stop=(f == FD - 1)) for f in range(FD)]
            p.op("te", mm, reads=[r_xn, r_gwT], writes=[pbres[4]])
            p.op("ve", lambda e: e.tensor_copy(out=rt[:, PRB:PRB + 32],
                                               in_=pb[4][:, 0:32]),
                 reads=[pbres[4]], writes=[r_probs])
            p.op("ve", lambda e: e.reduce_max(
                out=rt[:, COLS:COLS + 1], in_=rt[:, PRB:PRB + 32], axis=AX.X),
                reads=[r_probs], writes=[r_col])
            p.op("ve", lambda e: e.tensor_scalar(
                out=rt[:, PRB:PRB + 32], in0=rt[:, PRB:PRB + 32],
                scalar1=rt[:, COLS:COLS + 1], scalar2=None, op0=ALU.subtract),
                reads=[r_col], rmw=[r_probs])
            p.op("sc", lambda e: e.activation(
                out=rt[:, WRK:WRK + 32], in_=rt[:, PRB:PRB + 32],
                func=AF.Exp), reads=[r_probs], writes=[r_work])
            p.op("ve", lambda e: e.reduce_sum(
                out=rt[:, COLS:COLS + 1], in_=rt[:, WRK:WRK + 32], axis=AX.X),
                reads=[r_work], rmw=[r_col])
            p.op("ve", lambda e: e.reciprocal(
                out=rt[:, COLS + 1:COLS + 2], in_=rt[:, COLS:COLS + 1]),
                rmw=[r_col])
            p.op("ve", lambda e: e.tensor_scalar(
                out=rt[:, PRB:PRB + 32], in0=rt[:, WRK:WRK + 32],
                scalar1=rt[:, COLS + 1:COLS + 2], scalar2=None, op0=ALU.mult),
                reads=[r_work, r_col], rmw=[r_probs])
            p.op("ve", lambda e: e.max(out=rt[:, M8:M8 + 8],
                                       in_=rt[:, PRB:PRB + 32]),
                 reads=[r_probs], writes=[r_m8])
            p.op("ve", lambda e: e.reduce_sum(
                out=rt[:, COLS:COLS + 1], in_=rt[:, M8:M8 + 8], axis=AX.X),
                reads=[r_m8], rmw=[r_col])
            p.op("ve", lambda e: e.tensor_scalar(
                out=rt[:, COLS:COLS + 1], in0=rt[:, COLS:COLS + 1],
                scalar1=1e-12, scalar2=None, op0=ALU.max), rmw=[r_col])
            p.op("ve", lambda e: e.reciprocal(
                out=rt[:, COLS + 1:COLS + 2], in_=rt[:, COLS:COLS + 1]),
                rmw=[r_col])
            p.op("ve", lambda e: e.tensor_scalar(
                out=rt[:, REN:REN + 8], in0=rt[:, M8:M8 + 8],
                scalar1=rt[:, COLS + 1:COLS + 2], scalar2=None, op0=ALU.mult),
                reads=[r_m8, r_col], writes=[r_ren])
            p.op("ve", lambda e: e.tensor_tensor_scan(
                out=rt[:, CUM:CUM + 8], data0=ones[:, 0:8],
                data1=rt[:, REN:REN + 8], initial=0.0,
                op0=ALU.mult, op1=ALU.add),
                reads=[r_ren, r_ones], writes=[r_cum])
            p.op("ve", lambda e: e.memset(rt[:, ACT:ACT + 1], 0.0),
                 writes=[r_act])
            p.op("ve", lambda e: e.tensor_copy(
                out=rt[:, ACT + 1:ACT + 8], in_=rt[:, CUM:CUM + 7]),
                reads=[r_cum], rmw=[r_act])
            p.op("ve", lambda e: e.tensor_scalar(
                out=rt[:, ACT:ACT + 8], in0=rt[:, ACT:ACT + 8],
                scalar1=0.8, scalar2=None, op0=ALU.is_lt), rmw=[r_act])
            p.op("ve", lambda e: e.tensor_tensor(
                out=rt[:, RW:RW + 8], in0=rt[:, M8:M8 + 8],
                in1=rt[:, ACT:ACT + 8], op=ALU.mult),
                reads=[r_m8, r_act], writes=[r_rw])
            p.op("ve", lambda e: e.reduce_sum(
                out=rt[:, COLS:COLS + 1], in_=rt[:, RW:RW + 8], axis=AX.X),
                reads=[r_rw], rmw=[r_col])
            p.op("ve", lambda e: e.tensor_scalar(
                out=rt[:, COLS:COLS + 1], in0=rt[:, COLS:COLS + 1],
                scalar1=1e-12, scalar2=None, op0=ALU.max), rmw=[r_col])
            p.op("ve", lambda e: e.reciprocal(
                out=rt[:, COLS + 2:COLS + 3], in_=rt[:, COLS:COLS + 1]),
                rmw=[r_col])
            p.op("ve", lambda e: e.memset(rt[:, AMK:AMK + 32], 0.0),
                 writes=[r_amask])
            p.op("ve", lambda e: e.tensor_copy(out=rt[:, WRK:WRK + 32],
                                               in_=rt[:, PRB:PRB + 32]),
                 reads=[r_probs], writes=[r_work])
            for k in range(8):
                p.op("ve", lambda e, k=k: e.tensor_scalar(
                    out=rt[:, EQ:EQ + 32], in0=rt[:, WRK:WRK + 32],
                    scalar1=rt[:, M8 + k:M8 + k + 1], scalar2=None,
                    op0=ALU.is_equal),
                    reads=[r_work, r_m8], writes=[r_eq])
                p.op("ve", lambda e: e.tensor_tensor(
                    out=rt[:, EQ:EQ + 32], in0=rt[:, EQ:EQ + 32],
                    in1=iotam[:, :], op=ALU.mult),
                    reads=[r_iotam], rmw=[r_eq])
                p.op("ve", lambda e: e.tensor_reduce(
                    out=rt[:, COLS + 3:COLS + 4], in_=rt[:, EQ:EQ + 32],
                    axis=AX.X, op=ALU.min), reads=[r_eq], rmw=[r_col])
                p.op("ve", lambda e: e.tensor_scalar(
                    out=rt[:, EQ:EQ + 32], in0=iotam[:, :],
                    scalar1=rt[:, COLS + 3:COLS + 4], scalar2=None,
                    op0=ALU.is_equal),
                    reads=[r_iotam, r_col], rmw=[r_eq])
                p.op("ve", lambda e, k=k: e.scalar_tensor_tensor(
                    out=rt[:, AMK:AMK + 32], in0=rt[:, EQ:EQ + 32],
                    scalar=rt[:, ACT + k:ACT + k + 1],
                    in1=rt[:, AMK:AMK + 32], op0=ALU.mult, op1=ALU.add),
                    reads=[r_eq, r_act], rmw=[r_amask])
                p.op("ve", lambda e: e.scalar_tensor_tensor(
                    out=rt[:, WRK:WRK + 32], in0=rt[:, EQ:EQ + 32],
                    scalar=-2.0, in1=rt[:, WRK:WRK + 32],
                    op0=ALU.mult, op1=ALU.add),
                    reads=[r_eq], rmw=[r_work])
            p.op("ve", lambda e, tc=tc: e.scalar_tensor_tensor(
                out=rt[:, CEA + tc * 32:CEA + (tc + 1) * 32],
                in0=rt[:, PRB:PRB + 32], scalar=rt[:, COLS + 2:COLS + 3],
                in1=rt[:, AMK:AMK + 32], op0=ALU.mult, op1=ALU.mult),
                reads=[r_probs, r_col, r_amask],
                writes=[r_ce] if tc == 0 else (), rmw=[r_ce] if tc else ())
        for tc in range(8):
            bank = tc % 2
            p.op("te", lambda e, tc=tc, bank=bank: e.transpose(
                pb[bank][0:32, 0:128],
                rt[:, CEA + tc * 32:CEA + (tc + 1) * 32], id128[:, :]),
                reads=[r_ce, r_id], writes=[pbres[bank]])
            p.op("sc", lambda e, tc=tc, bank=bank: e.copy(
                out=ceT[0:32, tc * 128:(tc + 1) * 128],
                in_=pb[bank][0:32, 0:128]),
                reads=[pbres[bank]],
                writes=[r_ceT] if tc == 0 else (), rmw=[r_ceT] if tc else ())

        dmas = [lambda e, f=f: e.dma_start(
            out=acc[:, f * TOK:(f + 1) * TOK],
            in_=d_out[f * 128:(f + 1) * 128, :]) for f in range(FD)]
        p.op("sy", dmas, writes=[r_acc], dma="acc")

        cest = arena[0:1, O_CEST:O_CEST + TOK]

        for ex in range(NEXP):
            p.op("gp", lambda e, ex=ex: e.dma_start(
                out=cest, in_=ceT[ex:ex + 1, :]),
                reads=[r_ceT], writes=[r_cest], dma="ce")
            for th in range(2):
                p.op("te", lambda e, th=th: e.matmul(
                    pb[6][:, 0:512], ones[0:1, 0:128],
                    cest[:, th * 512:(th + 1) * 512], start=True, stop=True),
                    reads=[r_cest, r_ones], writes=[pbres[6]])
                p.op("ve", lambda e, th=th: e.tensor_copy(
                    out=ceB[:, th * 512:(th + 1) * 512], in_=pb[6][:, 0:512]),
                    reads=[pbres[6]],
                    writes=[r_ceB] if th == 0 else (),
                    rmw=[r_ceB] if th else ())
            for jc in range(JCC):
                for gu in range(2):
                    oc = jc + gu * 6
                    wb = oc % 2
                    p.op("sy", lambda e, ex=ex, oc=oc, wb=wb: e.dma_start(
                        out=wb4[:, wb * FD * 128:(wb + 1) * FD * 128]
                        .rearrange("p (k m) -> p k m", k=FD),
                        in_=d_wgu[ex:ex + 1, :, oc * 128:(oc + 1) * 128]
                        .rearrange("o (k p) m -> p (o k) m", p=128)),
                        writes=[r_wb4[wb]], dma="x%d" % wb)
                    for th in range(2):
                        bank = 2 * gu + th
                        mm = [lambda e, k=k, th=th, wb=wb, bank=bank:
                              e.matmul(
                            pb[bank][:, 0:512],
                            wb4[:, wb * FD * 128 + k * 128:
                                wb * FD * 128 + (k + 1) * 128],
                            xn[:, k * TOK + th * 512:k * TOK + (th + 1) * 512],
                            start=(k == 0), stop=(k == FD - 1))
                            for k in range(FD)]
                        p.op("te", mm, reads=[r_wb4[wb], r_xn],
                             writes=[pbres[bank]])
                        if gu == 0:
                            p.op("sc", lambda e, th=th, bank=bank:
                                 e.activation(
                                out=sg[:, th * 512:(th + 1) * 512],
                                in_=pb[bank][:, 0:512], func=AF.Silu),
                                reads=[pbres[bank]],
                                writes=[r_sg] if th == 0 else (),
                                rmw=[r_sg] if th else ())
                        else:
                            p.op("ve", lambda e, jc=jc, th=th, bank=bank:
                                 e.tensor_tensor(
                                out=hbuf[:, jc * TOK + th * 512:
                                         jc * TOK + (th + 1) * 512],
                                in0=pb[bank][:, 0:512],
                                in1=sg[:, th * 512:(th + 1) * 512],
                                op=ALU.mult),
                                reads=[pbres[bank], r_sg],
                                writes=[r_hb] if (jc == 0 and th == 0) and ex == 0
                                else (),
                                rmw=[r_hb] if not ((jc == 0 and th == 0)
                                                   and ex == 0) else ())
                            p.op("ve", lambda e, jc=jc, th=th:
                                 e.tensor_tensor(
                                out=hbuf[:, jc * TOK + th * 512:
                                         jc * TOK + (th + 1) * 512],
                                in0=hbuf[:, jc * TOK + th * 512:
                                         jc * TOK + (th + 1) * 512],
                                in1=ceB[:, th * 512:(th + 1) * 512],
                                op=ALU.mult),
                                reads=[r_ceB], rmw=[r_hb])
            for oc in range(FD):
                wb = oc % 2
                p.op("sy", lambda e, ex=ex, oc=oc, wb=wb: e.dma_start(
                    out=wb4[:, wb * FD * 128:wb * FD * 128 + JCC * 128]
                    .rearrange("p (k m) -> p k m", k=JCC),
                    in_=d_wd[ex:ex + 1, :, oc * 128:(oc + 1) * 128]
                    .rearrange("o (k p) m -> p (o k) m", p=128)),
                    writes=[r_wb4[wb]], dma="x%d" % wb)
                for th in range(2):
                    bank = 4 + th
                    mm = [lambda e, k=k, th=th, wb=wb, bank=bank: e.matmul(
                        pb[bank][:, 0:512],
                        wb4[:, wb * FD * 128 + k * 128:
                            wb * FD * 128 + (k + 1) * 128],
                        hbuf[:, k * TOK + th * 512:k * TOK + (th + 1) * 512],
                        start=(k == 0), stop=(k == JCC - 1))
                        for k in range(JCC)]
                    p.op("te", mm, reads=[r_wb4[wb], r_hb],
                         writes=[pbres[bank]])
                    p.op("ve", lambda e, oc=oc, th=th, bank=bank:
                         e.tensor_tensor(
                        out=acc[:, oc * TOK + th * 512:
                                oc * TOK + (th + 1) * 512],
                        in0=pb[bank][:, 0:512],
                        in1=acc[:, oc * TOK + th * 512:
                                oc * TOK + (th + 1) * 512], op=ALU.add),
                        reads=[pbres[bank]], rmw=[r_acc])
        dmas = [lambda e, f=f: e.dma_start(
            out=d_out[f * 128:(f + 1) * 128, :],
            in_=acc[:, f * TOK:(f + 1) * TOK]) for f in range(FD)]
        p.op("gp", dmas, reads=[r_acc], dma="st")
        final = p.cnt["d_st"]
        p.raw("gp", lambda e, v=final: e.wait_ge(p.sem["d_st"], v))

        block = ctx.enter_context(nc.Block())
        p.flush(block)
    return nc


_NC_CACHE = [None]


def kernel(L0_attn_u, L0_attn_W_in, L0_attn_conv_w, L0_attn_conv_b,
           L0_attn_dt_bias, L0_attn_A_log, L0_attn_Dp, L0_attn_norm_w,
           L0_attn_W_out, L0_ffn_gate_weight, L0_ffn_gate_up_proj,
           L0_ffn_down_proj):
    u = np.asarray(L0_attn_u, dtype=np.float32)
    w_in = np.ascontiguousarray(np.asarray(L0_attn_W_in, np.float32))
    w_out = np.ascontiguousarray(np.asarray(L0_attn_W_out, np.float32))
    conv_w = np.asarray(L0_attn_conv_w, np.float32)
    conv_b = np.asarray(L0_attn_conv_b, np.float32)
    dtb = np.ascontiguousarray(np.asarray(L0_attn_dt_bias,
                                          np.float32).reshape(64, 1))
    alog = np.ascontiguousarray(np.asarray(L0_attn_A_log,
                                           np.float32).reshape(64, 1))
    dp = np.asarray(L0_attn_Dp, np.float32)
    normw = np.asarray(L0_attn_norm_w, np.float32)
    gw = np.asarray(L0_ffn_gate_weight, np.float32)
    wgu = np.asarray(L0_ffn_gate_up_proj, np.float32)
    wd = np.asarray(L0_ffn_down_proj, np.float32)

    convw_l = np.ascontiguousarray(
        conv_w.reshape(CONVCC, 128, 4).transpose(1, 0, 2).reshape(128, -1))
    convb_l = np.ascontiguousarray(conv_b.reshape(CONVCC, 128).T)
    dpf_l = np.ascontiguousarray(np.broadcast_to(dp[None, :], (128, H)))
    normw_l = np.ascontiguousarray(normw.reshape(ZCC, 128).T)
    gwT_l = np.ascontiguousarray(gw.T)
    wgu_l = np.ascontiguousarray(wgu.transpose(0, 2, 1))
    wd_l = np.ascontiguousarray(wd.transpose(0, 2, 1))

    in_maps = []
    for c in range(8):
        b, j = divmod(c, 4)
        npad = 3072 - 1024 * j
        uT = np.concatenate([
            np.zeros((DM, npad), np.float32),
            np.ascontiguousarray(u[b, :1024 * j, :].T),
            np.ascontiguousarray(u[b, 1024 * j:1024 * (j + 1), :].T)], axis=1)
        vmask = np.zeros((NCH,), np.float32)
        vmask[OWN0 - 4 * j:] = 1.0
        valid_l = np.ascontiguousarray(
            np.broadcast_to(vmask[None, :], (128, NCH)))
        inval_l = np.ascontiguousarray(1.0 - valid_l)
        in_maps.append({
            "uT": np.ascontiguousarray(uT),
            "w_in": w_in, "w_out": w_out,
            "convw": convw_l, "convb": convb_l,
            "dtb": dtb, "alog": alog, "dpf": dpf_l, "normw": normw_l,
            "gwT": gwT_l, "wgu": wgu_l, "wd": wd_l,
            "valid": valid_l, "invalid": inval_l,
        })

    if _NC_CACHE[0] is None:
        _NC_CACHE[0] = build_program()
    nc = _NC_CACHE[0]
    res = run_bass_kernel_spmd(nc, in_maps, core_ids=list(range(8)))
    out = np.empty((2, 4096, DM), np.float32)
    for c in range(8):
        b, j = divmod(c, 4)
        out[b, 1024 * j:1024 * (j + 1), :] = np.asarray(
            res.results[c]["out"]).T
    return out


# version 03
# speedup vs baseline: 1.1171x; 1.0265x over previous
import numpy as np
from contextlib import ExitStack

import concourse.bass as bass
import concourse.mybir as mybir
from concourse.bass_utils import run_bass_kernel_spmd

F32 = mybir.dt.float32
ALU = mybir.AluOpType
AF = mybir.ActivationFunctionType
AX = mybir.AxisListType

CH = 256
NCH = 16
OWN0 = 12
H = 64
HD = 64
DM = 2048
FD = DM // 128
DIN = 4096
ZCC = DIN // 128
CONVCC = 34
NEXP = 32
INTER = 768
JCC = INTER // 128
TOK = 1024

ENG_METHOD = {"te": "tensor", "ve": "vector", "sc": "scalar",
              "gp": "gpsimd", "sy": "sync"}

TAPS = False


class Res:
    __slots__ = ("w", "r")

    def __init__(self):
        self.w = None
        self.r = []


class Prog:
    def __init__(self, nc, ctx):
        self.nc = nc
        self.ctx = ctx
        self.q = {k: [] for k in ENG_METHOD}
        self.sem = {}
        self.cnt = {}
        for k in ENG_METHOD:
            self.sem[k] = ctx.enter_context(nc.semaphore("sem_" + k))
            self.cnt[k] = 0
        self.lastwait = {}
        self.fence_state = None
        self.fence_absorbed = set()

    def dsem(self, name):
        key = "d_" + name
        if key not in self.sem:
            self.sem[key] = self.ctx.enter_context(
                self.nc.semaphore("sem" + key))
            self.cnt[key] = 0
        return key

    def _wait(self, en, dep):
        sem, val = dep
        if sem.startswith("d_"):
            val = max(val, self.cnt[sem])

        key = (en, sem)
        if self.lastwait.get(key, 0) >= val:
            return
        self.lastwait[key] = val
        s = self.sem[sem]
        self.q[en].append(lambda e, s=s, v=val: e.wait_ge(s, v))

    def op(self, en, build, reads=(), writes=(), rmw=(), dma=False):
        if self.fence_state is not None and en not in self.fence_absorbed:
            self.fence_absorbed.add(en)
            for sem, val in self.fence_state.items():
                if val > 0:
                    self._wait(en, (sem, val))
        deps = []
        for r in reads:
            if r.w:
                deps.append(r.w)
        for r in tuple(writes) + tuple(rmw):
            if r.w:
                deps.append(r.w)
            deps.extend(r.r)
        byname = {}
        for s, v in deps:
            if byname.get(s, 0) < v:
                byname[s] = v
        for s, v in byname.items():
            self._wait(en, (s, v))
        builders = build if isinstance(build, (list, tuple)) else [build]
        semn = self.dsem(dma) if dma else en
        s = self.sem[semn]
        if dma:
            for b in builders:
                self.cnt[semn] += 16
                self.q[en].append(lambda e, b=b, s=s: b(e).then_inc(s, 16))
        else:
            for b in builders[:-1]:
                self.q[en].append(lambda e, b=b: b(e))
            self.cnt[semn] += 1
            self.q[en].append(
                lambda e, b=builders[-1], s=s: b(e).then_inc(s, 1))
        tok = (semn, self.cnt[semn])
        for r in reads:
            rr = [t for t in r.r if t[0] != tok[0]]
            rr.append(tok)
            r.r = rr
        for r in tuple(writes) + tuple(rmw):
            r.w = tok
            r.r = []
        return tok

    def fence(self):
        self.fence_state = dict(self.cnt)
        self.fence_absorbed = set()

    def raw(self, en, closure):
        self.q[en].append(closure)

    def flush(self, block):
        for en, meth in ENG_METHOD.items():
            lst = self.q[en]
            if not lst:
                continue

            def runner(e, lst=lst):
                for f in lst:
                    f(e)
            getattr(block, meth)(runner)
            self.q[en] = []


def build_program():
    nc = bass.Bass("TRN2", target_bir_lowering=False)

    d_uT = nc.dram_tensor("uT", [DM, NCH * CH], F32, kind="ExternalInput").ap()
    d_win = nc.dram_tensor("w_in", [DM, 8512], F32, kind="ExternalInput").ap()
    d_wout = nc.dram_tensor("w_out", [DIN, DM], F32, kind="ExternalInput").ap()
    d_convw = nc.dram_tensor("convw", [128, CONVCC * 4], F32,
                             kind="ExternalInput").ap()
    d_convb = nc.dram_tensor("convb", [128, CONVCC], F32,
                             kind="ExternalInput").ap()
    d_dtb = nc.dram_tensor("dtb", [64, 1], F32, kind="ExternalInput").ap()
    d_alog = nc.dram_tensor("alog", [64, 1], F32, kind="ExternalInput").ap()
    d_dpf = nc.dram_tensor("dpf", [128, H], F32, kind="ExternalInput").ap()
    d_normw = nc.dram_tensor("normw", [128, ZCC], F32,
                             kind="ExternalInput").ap()
    d_gwT = nc.dram_tensor("gwT", [DM, NEXP], F32, kind="ExternalInput").ap()
    d_wgu = nc.dram_tensor("wgu", [NEXP, DM, 1536], F32,
                           kind="ExternalInput").ap()
    d_wd = nc.dram_tensor("wd", [NEXP, INTER, DM], F32,
                          kind="ExternalInput").ap()
    d_valid = nc.dram_tensor("valid", [128, NCH], F32,
                             kind="ExternalInput").ap()
    d_inval = nc.dram_tensor("invalid", [128, NCH], F32,
                             kind="ExternalInput").ap()
    d_out = nc.dram_tensor("out", [DM, TOK], F32, kind="ExternalOutput").ap()
    d_taps = {}
    if TAPS:
        for ci in range(OWN0, NCH):
            d_taps["S%d" % ci] = nc.dram_tensor(
                "tapS%d" % ci, [128, DIN], F32, kind="ExternalOutput").ap()
            d_taps["e%d" % ci] = nc.dram_tensor(
                "tape%d" % ci, [128, 64], F32, kind="ExternalOutput").ap()

    winr = d_win.rearrange("(k p) m -> p k m", p=128)
    woutr = d_wout.rearrange("(k p) m -> p k m", p=128)
    gwTr = d_gwT.rearrange("(k p) m -> p k m", p=128)

    with ExitStack() as ctx:
        p = Prog(nc, ctx)

        def sb(name, cols, rows=128):
            t = ctx.enter_context(nc.sbuf_tensor(name, [rows, cols], F32))
            return t, Res()

        ones, r_ones = sb("ones", 256)
        id128, r_id = sb("id128", 128)
        pen, r_pen = sb("pen", 2 * CH)
        identD, r_identD = sb("identD", 2 * CH)
        iotam, r_iotam = sb("iotam", 32)
        csm, r_csm = sb("csm", 8)
        valid, r_valid = sb("validsb", NCH)
        inval, r_inval = sb("invalsb", NCH)
        dtb, r_dtb = sb("dtbsb", 1, rows=64)
        alog, r_alog = sb("alogsb", 1, rows=64)
        acol, r_acol = sb("acolsb", 1, rows=64)
        dpf, r_dpf = sb("dpfsb", H)
        normw, r_normw = sb("normwsb", ZCC)
        convw, r_convw = sb("convwsb", CONVCC * 4)
        convb, r_convb = sb("convbsb", CONVCC)
        gwT, r_gwT = sb("gwTsb", FD * NEXP)
        rows, r_rows = sb("rowsbuf", 1280)
        ysq, _r_ysq0 = sb("ysqbuf", 2 * CH)
        r_ysq = [_r_ysq0, Res()]
        ebT, r_ebT = sb("ebTsb", H)
        erow, r_erow = sb("erowsb", H)
        ecol, r_ecol = sb("ecolsb", 2, rows=64)

        pb = []
        pbres = []
        for i in range(8):
            t = ctx.enter_context(nc.psum_tensor("pb%d" % i, [128, 512], F32))
            pb.append(t)
            pbres.append(Res())

        AR_COLS = 47760
        arena = ctx.enter_context(nc.sbuf_tensor("arena", [128, AR_COLS],
                                                 F32))

        def seg(off, cols, rows=128):
            return arena[0:rows, off:off + cols]

        o = 0

        def nxt(cols):
            nonlocal o
            a = o
            o += cols
            return a
        O_UC = nxt(FD * CH)
        O_XNC = nxt(FD * CH)
        O_RAWXT = nxt(CONVCC * 259)
        O_XBC = nxt(CONVCC * CH)
        O_HALO = nxt(CONVCC * 3)
        O_DTS = nxt(6 * CH)
        O_TTS = nxt(3 * 128)
        O_BT = nxt(2 * 128)
        O_G2 = nxt(2 * CH)
        O_MH = nxt(2 * CH)
        O_M0 = nxt(2 * CH)
        O_TT = nxt(2 * CH)
        O_CX = nxt(2 * CH)
        O_ET1 = nxt(CH)
        O_CONVT = nxt(2 * CH)
        O_BS = nxt(2 * 128)
        O_WBUF = nxt(2 * FD * 128)
        O_S = nxt(DIN)
        O_X2 = nxt(2 * CH + 2 * CH + 2 * CH + 2 * CH + CH + 2 * 128
                   + 2 * CH + 4 * 64 + 2 * CH)
        assert o <= AR_COLS, o

        u_c = seg(O_UC, FD * CH)
        xn_c = seg(O_XNC, FD * CH)
        rawXT = seg(O_RAWXT, CONVCC * 259)
        xbc = seg(O_XBC, CONVCC * CH)
        halob = seg(O_HALO, CONVCC * 3)
        dts = seg(O_DTS, 6 * CH, rows=64)
        tts = seg(O_TTS, 3 * 128)
        BT = seg(O_BT, 2 * 128)
        G2sb = seg(O_G2, 2 * CH)
        mh = seg(O_MH, 2 * CH)
        m0 = seg(O_M0, 2 * CH)
        ttmp = seg(O_TT, 2 * CH)
        cexp = seg(O_CX, 2 * CH)
        expT1 = seg(O_ET1, CH)
        convt = seg(O_CONVT, 2 * CH)
        Bs = seg(O_BS, 2 * 128)
        wbuf = seg(O_WBUF, 2 * FD * 128)
        S = seg(O_S, DIN)
        x2 = O_X2
        mh_b = [mh, seg(x2, 2 * CH)]
        m0_b = [m0, seg(x2 + 2 * CH, 2 * CH)]
        ttmp_b = [ttmp, seg(x2 + 4 * CH, 2 * CH)]
        cexp_b = [cexp, seg(x2 + 6 * CH, 2 * CH)]
        expT1_b = [expT1, seg(x2 + 8 * CH, CH)]
        Bs_b = [Bs, seg(x2 + 9 * CH, 2 * 128)]
        t1sb_b = [seg(x2 + 10 * CH, CH), seg(x2 + 11 * CH, CH)]
        Xs_b = [seg(x2 + 12 * CH, 128), seg(x2 + 12 * CH + 128, 128)]
        mrb = seg(x2 + 12 * CH + 256, 2 * CH)

        r_uc, r_xnc = Res(), Res()
        r_raw = [Res() for _ in range(CONVCC)]
        r_XT = Res()
        r_xbcx, r_xbcBC = Res(), Res()
        r_halob = Res()
        r_dtraw, r_dt, r_dA, r_acum, r_dtds = (Res() for _ in range(5))
        r_nAT, r_dtT, r_dtdsT = Res(), Res(), Res()
        r_BT, r_G2 = Res(), Res()
        r_mh = [Res(), Res()]
        r_expT1 = [Res(), Res()]
        r_t1sb = [Res(), Res()]
        r_Xs = [Res(), Res()]
        r_mrb = Res()
        r_m0 = [[Res(), Res()], [Res(), Res()]]
        r_ttmp = [[Res(), Res()], [Res(), Res()]]
        r_cexp = [Res(), Res()]
        r_convt = [Res(), Res()]
        r_Bs = [[Res(), Res()], [Res(), Res()]]
        r_wbuf = [Res(), Res()]
        r_S = Res()

        for dst, src, res in [(valid, d_valid, r_valid),
                              (inval, d_inval, r_inval),
                              (dtb, d_dtb, r_dtb), (alog, d_alog, r_alog),
                              (dpf, d_dpf, r_dpf), (normw, d_normw, r_normw),
                              (convw, d_convw, r_convw),
                              (convb, d_convb, r_convb)]:
            p.op("sy", lambda e, dst=dst, src=src: e.dma_start(
                out=dst[:, :], in_=src), writes=[res], dma="cst")
        p.op("sy", lambda e: e.dma_start(
            out=gwT[:, :].rearrange("p (k m) -> p k m", k=FD), in_=gwTr),
            writes=[r_gwT], dma="cst")

        p.op("ve", lambda e: e.memset(ones[:, :], 1.0), writes=[r_ones])
        p.op("ve", lambda e: e.memset(csm[:, 0:1], 1.0), writes=[r_csm])
        p.op("ve", lambda e: e.memset(csm[:, 1:2], 1e-5), rmw=[r_csm])
        p.op("gp", lambda e: e.iota(id128[:, :], pattern=[[1, 128]], base=0,
                                    channel_multiplier=-1,
                                    allow_small_or_imprecise_dtypes=True),
             writes=[r_id])
        p.op("gp", lambda e: e.tensor_scalar(out=id128[:, :], in0=id128[:, :],
                                             scalar1=0.0, scalar2=None,
                                             op0=ALU.is_equal), rmw=[r_id])
        for k in range(2):
            p.op("gp", lambda e, k=k: e.iota(
                pen[:, k * CH:(k + 1) * CH], pattern=[[1, CH]], base=-128 * k,
                channel_multiplier=-1, allow_small_or_imprecise_dtypes=True),
                rmw=[r_pen])
            p.op("gp", lambda e, k=k: e.tensor_scalar(
                out=identD[:, k * CH:(k + 1) * CH],
                in0=pen[:, k * CH:(k + 1) * CH],
                scalar1=0.0, scalar2=None, op0=ALU.is_equal),
                reads=[r_pen], rmw=[r_identD])
            p.op("gp", lambda e, k=k: e.tensor_scalar(
                out=pen[:, k * CH:(k + 1) * CH],
                in0=pen[:, k * CH:(k + 1) * CH],
                scalar1=0.0, scalar2=-1e30, op0=ALU.is_lt, op1=ALU.mult),
                rmw=[r_pen])
        p.op("gp", lambda e: e.iota(iotam[:, :], pattern=[[1, 32]], base=-999,
                                    channel_multiplier=0,
                                    allow_small_or_imprecise_dtypes=True),
             writes=[r_iotam])
        p.op("sc", lambda e: e.activation(out=acol[:, :], in_=alog[:, :],
                                          func=AF.Exp),
             reads=[r_alog], writes=[r_acol])
        p.op("ve", lambda e: e.tensor_scalar(out=acol[:, :], in0=acol[:, :],
                                             scalar1=-1.0, scalar2=None,
                                             op0=ALU.mult), rmw=[r_acol])
        p.op("ve", lambda e: e.memset(S[:, :], 0.0), writes=[r_S])
        p.op("ve", lambda e: e.memset(halob[:, :], 0.0), writes=[r_halob])

        def emit_ln(ci, xn_ap_fn, r_xn_dest, u_buf, r_ubuf, usem="u"):
            dmas = [lambda e, f=f, ci=ci: e.dma_start(
                out=u_buf[:, f * CH:(f + 1) * CH],
                in_=d_uT[f * 128:(f + 1) * 128, ci * CH:(ci + 1) * CH])
                for f in range(FD)]
            p.op("sy", dmas, writes=[r_ubuf], dma=usem)
            for f in range(FD):
                ys = f % 2
                p.op("sc", lambda e, f=f, ys=ys: e.square(
                    out=ysq[:, ys * CH:(ys + 1) * CH],
                    in_=u_buf[:, f * CH:(f + 1) * CH]),
                    reads=[r_ubuf], writes=[r_ysq[ys]])
                p.op("te", lambda e, f=f: e.matmul(
                    pb[6][0:1, 0:CH], ones[:, 0:1],
                    u_buf[:, f * CH:(f + 1) * CH],
                    start=(f == 0), stop=(f == FD - 1)),
                    reads=[r_ubuf, r_ones],
                    writes=[pbres[6]] if f == 0 else (),
                    rmw=[pbres[6]] if f else ())
                p.op("te", lambda e, f=f, ys=ys: e.matmul(
                    pb[7][0:1, 0:CH], ones[:, 0:1],
                    ysq[:, ys * CH:(ys + 1) * CH],
                    start=(f == 0), stop=(f == FD - 1)),
                    reads=[r_ysq[ys], r_ones],
                    writes=[pbres[7]] if f == 0 else (),
                    rmw=[pbres[7]] if f else ())
            p.op("sc", lambda e: e.copy(out=rows[0:1, 0:CH],
                                        in_=pb[6][0:1, 0:CH]),
                 reads=[pbres[6]], rmw=[r_rows])
            p.op("sc", lambda e: e.copy(out=rows[0:1, CH:2 * CH],
                                        in_=pb[7][0:1, 0:CH]),
                 reads=[pbres[7]], rmw=[r_rows])
            p.op("ve", lambda e: e.tensor_scalar(
                out=rows[0:1, 2 * CH:3 * CH], in0=rows[0:1, 0:CH],
                scalar1=1.0 / DM, scalar2=None, op0=ALU.mult), rmw=[r_rows])
            p.op("ve", lambda e: e.tensor_scalar(
                out=rows[0:1, CH:2 * CH], in0=rows[0:1, CH:2 * CH],
                scalar1=1.0 / DM, scalar2=None, op0=ALU.mult), rmw=[r_rows])
            p.op("ve", lambda e: e.tensor_tensor(
                out=rows[0:1, 0:CH], in0=rows[0:1, 2 * CH:3 * CH],
                in1=rows[0:1, 2 * CH:3 * CH], op=ALU.mult), rmw=[r_rows])
            p.op("ve", lambda e: e.tensor_tensor(
                out=rows[0:1, 0:CH], in0=rows[0:1, CH:2 * CH],
                in1=rows[0:1, 0:CH], op=ALU.subtract), rmw=[r_rows])
            p.op("sc", lambda e: e.activation(
                out=rows[0:1, 4 * CH:5 * CH], in_=rows[0:1, 0:CH],
                func=AF.Sqrt, bias=csm[0:1, 1:2], scale=1.0),
                reads=[r_csm], rmw=[r_rows])
            p.op("ve", lambda e: e.reciprocal(
                out=rows[0:1, 3 * CH:4 * CH], in_=rows[0:1, 4 * CH:5 * CH]),
                rmw=[r_rows])
            p.op("te", lambda e: e.matmul(
                pb[6][:, 0:CH], ones[0:1, 0:128], rows[0:1, 2 * CH:3 * CH],
                start=True, stop=True),
                reads=[r_rows, r_ones], writes=[pbres[6]])
            p.op("te", lambda e: e.matmul(
                pb[7][:, 0:CH], ones[0:1, 0:128], rows[0:1, 3 * CH:4 * CH],
                start=True, stop=True),
                reads=[r_rows, r_ones], writes=[pbres[7]])
            p.op("sc", lambda e: e.copy(out=mrb[:, 0:CH], in_=pb[6][:, 0:CH]),
                 reads=[pbres[6]], writes=[r_mrb])
            p.op("sc", lambda e: e.copy(out=mrb[:, CH:2 * CH],
                                        in_=pb[7][:, 0:CH]),
                 reads=[pbres[7]], rmw=[r_mrb])
            for f in range(FD):
                ap = xn_ap_fn(f)
                en2 = "ve" if f % 2 == 0 else "gp"
                p.op(en2, lambda e, f=f, ap=ap: e.tensor_tensor(
                    out=ap, in0=u_buf[:, f * CH:(f + 1) * CH],
                    in1=mrb[:, 0:CH], op=ALU.subtract),
                    reads=[r_mrb, r_ubuf],
                    writes=[r_xn_dest] if f == 0 else (),
                    rmw=[r_xn_dest] if f else ())
                p.op(en2, lambda e, ap=ap: e.tensor_tensor(
                    out=ap, in0=ap, in1=mrb[:, CH:2 * CH], op=ALU.mult),
                    reads=[r_mrb], rmw=[r_xn_dest])

        def load_w_slice(dst_off, src_ap, res, nk, width, buf, sem):
            p.op("sy", lambda e: e.dma_start(
                out=buf[:, dst_off:dst_off + nk * width].rearrange(
                    "p (k m) -> p k m", k=nk),
                in_=src_ap), writes=[res], dma=sem)

        def emit_inproj(ci, mlist, rhs_ap, r_rhs):
            for i, (c0, width, kind, idx) in enumerate(mlist):
                wb = i % 2
                bank = i % 2
                load_w_slice(wb * FD * 128, winr[:, :, c0:c0 + width],
                             r_wbuf[wb], FD, width, wbuf, "w%d" % wb)
                mm = [lambda e, k=k, wb=wb, bank=bank, width=width: e.matmul(
                    pb[bank][0:width, 0:CH],
                    wbuf[:, wb * FD * 128 + k * width:
                         wb * FD * 128 + (k + 1) * width],
                    rhs_ap[:, k * CH:(k + 1) * CH],
                    start=(k == 0), stop=(k == FD - 1)) for k in range(FD)]
                p.op("te", mm, reads=[r_wbuf[wb], r_rhs],
                     writes=[pbres[bank]])
                if kind == "raw":
                    cc = idx
                    p.op("gp", lambda e, cc=cc: e.tensor_copy(
                        out=rawXT[:, cc * 259:cc * 259 + 3],
                        in_=halob[:, cc * 3:cc * 3 + 3]),
                        reads=[r_halob], writes=[r_raw[cc]], rmw=[r_XT])
                    p.op("sc", lambda e, cc=cc, bank=bank: e.copy(
                        out=rawXT[:, cc * 259 + 3:cc * 259 + 259],
                        in_=pb[bank][:, 0:CH]),
                        reads=[pbres[bank]], rmw=[r_raw[cc], r_XT])
                elif kind == "dt":
                    p.op("sc", lambda e, bank=bank: e.copy(
                        out=dts[:, 0:CH], in_=pb[bank][0:64, 0:CH]),
                        reads=[pbres[bank]], writes=[r_dtraw])
                elif kind == "z":
                    zc = idx
                    p.op("sc", lambda e, zc=zc, bank=bank: e.activation(
                        out=xbc[:, zc * CH:(zc + 1) * CH],
                        in_=pb[bank][:, 0:CH], func=AF.Silu),
                        reads=[pbres[bank]],
                        writes=[r_xbcx] if zc == 0 else (),
                        rmw=[r_xbcx] if zc else ())

        def emit_conv(ci, ncc):
            for cc in [32, 33] + list(range(32)):
                ct = cc % 2
                p.op("ve", lambda e, cc=cc, ct=ct: e.tensor_scalar(
                    out=convt[:, ct * CH:(ct + 1) * CH],
                    in0=rawXT[:, cc * 259 + 0:cc * 259 + CH],
                    scalar1=convw[:, cc * 4:cc * 4 + 1], scalar2=None,
                    op0=ALU.mult),
                    reads=[r_raw[cc], r_convw], writes=[r_convt[ct]])
                for k in range(1, 4):
                    p.op("ve", lambda e, cc=cc, k=k, ct=ct:
                         e.scalar_tensor_tensor(
                        out=convt[:, ct * CH:(ct + 1) * CH],
                        in0=rawXT[:, cc * 259 + k:cc * 259 + k + CH],
                        scalar=convw[:, cc * 4 + k:cc * 4 + k + 1],
                        in1=convt[:, ct * CH:(ct + 1) * CH],
                        op0=ALU.mult, op1=ALU.add),
                        reads=[r_raw[cc], r_convw], rmw=[r_convt[ct]])
                tgt = (r_xbcx if cc < 32 else r_xbcBC)
                p.op("sc", lambda e, cc=cc, ct=ct: e.activation(
                    out=xbc[:, cc * CH:(cc + 1) * CH],
                    in_=convt[:, ct * CH:(ct + 1) * CH], func=AF.Silu,
                    bias=convb[:, cc:cc + 1], scale=1.0),
                    reads=[r_convt[ct], r_convb],
                    writes=[tgt] if cc in (0, 32) else (),
                    rmw=[tgt] if cc not in (0, 32) else ())
            if ci < NCH - 1:
                for cc in [32, 33] + list(range(32)):
                    p.op("gp", lambda e, cc=cc: e.tensor_copy(
                        out=halob[:, cc * 3:cc * 3 + 3],
                        in_=rawXT[:, cc * 259 + CH:cc * 259 + CH + 3]),
                        reads=[r_raw[cc]], rmw=[r_halob])

        def emit_dt(ci, own):
            p.op("sc", lambda e: e.activation(
                out=dts[:, CH:2 * CH], in_=dts[:, 0:CH], func=AF.Exp,
                bias=dtb[:, 0:1], scale=1.0),
                reads=[r_dtraw, r_dtb], writes=[r_dt])
            p.op("sc", lambda e: e.activation(
                out=dts[:, 5 * CH:6 * CH], in_=dts[:, CH:2 * CH], func=AF.Ln,
                bias=csm[0:64, 0:1], scale=1.0),
                reads=[r_csm], rmw=[r_dt])
            p.op("ve", lambda e: e.tensor_scalar(
                out=dts[:, 2 * CH:3 * CH], in0=dts[:, 5 * CH:6 * CH],
                scalar1=acol[:, 0:1], scalar2=None, op0=ALU.mult),
                reads=[r_dt, r_acol], writes=[r_dA])
            p.op("ve", lambda e: e.tensor_tensor_scan(
                out=dts[:, 3 * CH:4 * CH], data0=ones[0:64, 0:CH],
                data1=dts[:, 2 * CH:3 * CH], initial=0.0,
                op0=ALU.mult, op1=ALU.add),
                reads=[r_dA, r_ones], writes=[r_acum])
            p.op("sc", lambda e: e.activation(
                out=ecol[:, 0:1], in_=dts[:, 4 * CH - 1:4 * CH],
                func=AF.Exp), reads=[r_acum], writes=[r_ecol])
            p.op("ve", lambda e, ci=ci: e.scalar_tensor_tensor(
                out=ecol[:, 1:2], in0=ecol[:, 0:1],
                scalar=valid[0:64, ci:ci + 1], in1=inval[0:64, ci:ci + 1],
                op0=ALU.mult, op1=ALU.add),
                reads=[r_valid, r_inval], rmw=[r_ecol])
            p.op("sc", lambda e: e.activation(
                out=dts[:, 4 * CH:5 * CH], in_=dts[:, 3 * CH:4 * CH],
                func=AF.Exp, bias=dts[:, 4 * CH - 1:4 * CH], scale=-1.0),
                reads=[r_acum], writes=[r_dtds])
            p.op("ve", lambda e: e.tensor_tensor(
                out=dts[:, 4 * CH:5 * CH], in0=dts[:, 4 * CH:5 * CH],
                in1=dts[:, 5 * CH:6 * CH], op=ALU.mult),
                reads=[r_dt], rmw=[r_dtds])

        def emit_transposes(ci, own):
            for k in range(2):
                bank = 2 + k
                p.op("te", lambda e, k=k, bank=bank: e.transpose(
                    pb[bank][:, 0:128],
                    xbc[:, 32 * CH + k * 128:32 * CH + (k + 1) * 128],
                    id128[:, :]),
                    reads=[r_xbcBC, r_id], writes=[pbres[bank]])
                p.op("sc", lambda e, k=k, bank=bank: e.copy(
                    out=BT[:, k * 128:(k + 1) * 128], in_=pb[bank][:, 0:128]),
                    reads=[pbres[bank]],
                    writes=[r_BT] if k == 0 else (), rmw=[r_BT] if k else ())
            specs = [(4, r_dtds, 256, 1.0, r_dtdsT)]
            if own:
                specs.append((3, r_acum, 0, -1.0, r_nAT))
                specs.append((5, r_dt, 128, 1.0, r_dtT))
            for (so, rsrc, do, scl, rdst) in specs:
                for k in range(2):
                    bank = 2 + k
                    p.op("te", lambda e, so=so, k=k, bank=bank: e.transpose(
                        pb[bank][:, 0:64],
                        dts[:, so * CH + k * 128:so * CH + (k + 1) * 128],
                        id128[0:64, 0:64]),
                        reads=[rsrc, r_id], writes=[pbres[bank]])
                    if scl == 1.0:
                        p.op("sc", lambda e, do=do, k=k, bank=bank: e.copy(
                            out=tts[:, do + k * 64:do + (k + 1) * 64],
                            in_=pb[bank][:, 0:64]),
                            reads=[pbres[bank]],
                            writes=[rdst] if k == 0 else (),
                            rmw=[rdst] if k else ())
                    else:
                        p.op("ve", lambda e, do=do, k=k, bank=bank:
                             e.tensor_scalar(
                            out=tts[:, do + k * 64:do + (k + 1) * 64],
                            in0=pb[bank][:, 0:64], scalar1=-1.0, scalar2=None,
                            op0=ALU.mult),
                            reads=[pbres[bank]],
                            writes=[rdst] if k == 0 else (),
                            rmw=[rdst] if k else ())
            first = [True]
            for f in range(ZCC):
                for k in range(2):
                    i = f * 2 + k
                    bank = i % 4
                    p.op("te", lambda e, f=f, k=k, bank=bank: e.transpose(
                        pb[bank][:, 0:128],
                        xbc[:, f * CH + k * 128:f * CH + (k + 1) * 128],
                        id128[:, :]),
                        reads=[r_xbcx, r_id], writes=[pbres[bank]])
                    if first[0]:
                        first[0] = False
                        wl, rl = [r_XT] + r_raw, ()
                    else:
                        wl, rl = (), [r_XT]
                    en2 = "sc" if i % 2 == 0 else "ve"
                    if en2 == "sc":
                        p.op("sc", lambda e, f=f, k=k, bank=bank: e.copy(
                            out=rawXT[:, k * DIN + f * 128:
                                      k * DIN + (f + 1) * 128],
                            in_=pb[bank][:, 0:128]),
                            reads=[pbres[bank]], writes=wl, rmw=rl)
                    else:
                        p.op("ve", lambda e, f=f, k=k, bank=bank:
                             e.tensor_copy(
                            out=rawXT[:, k * DIN + f * 128:
                                      k * DIN + (f + 1) * 128],
                            in_=pb[bank][:, 0:128]),
                            reads=[pbres[bank]], writes=wl, rmw=rl)

        def emit_G2(ci):
            mm = [lambda e, k=k: e.matmul(
                pb[4][:, k * CH:(k + 1) * CH],
                xbc[:, 32 * CH + k * 128:32 * CH + (k + 1) * 128],
                xbc[:, 33 * CH:34 * CH], start=True, stop=True)
                for k in range(2)]
            p.op("te", mm, reads=[r_xbcBC], writes=[pbres[4]])
            for k in range(2):
                p.op("ve", lambda e, k=k: e.tensor_copy(
                    out=G2sb[:, k * CH:(k + 1) * CH],
                    in_=pb[4][:, k * CH:(k + 1) * CH]),
                    reads=[pbres[4]],
                    writes=[r_G2] if k == 0 else (), rmw=[r_G2] if k else ())

        def emit_Y(ci):
            def t1(h):
                bank = 4 + (h % 2)
                p.op("te", lambda e, h=h, bank=bank: e.matmul(
                    pb[bank][:, 0:CH],
                    id128[0:64, h:h + 1].to_broadcast([64, 128]),
                    dts[:, 3 * CH:4 * CH], start=True, stop=True),
                    reads=[r_acum, r_id], writes=[pbres[bank]])
            t1(0)
            for h in range(H):
                bank = 4 + (h % 2)
                bb = h % 2
                p.op("sc", lambda e, bb=bb, bank=bank: e.activation(
                    out=expT1_b[bb][:, :], in_=pb[bank][:, 0:CH],
                    func=AF.Exp),
                    reads=[pbres[bank]], writes=[r_expT1[bb]])
                p.op("sc", lambda e, bb=bb, bank=bank: e.copy(
                    out=t1sb_b[bb][:, :], in_=pb[bank][:, 0:CH]),
                    reads=[pbres[bank]], writes=[r_t1sb[bb]])
                p.op("gp", lambda e, bb=bb: e.tensor_tensor(
                    out=cexp_b[bb][:, 0:CH],
                    in0=xbc[:, 33 * CH:34 * CH], in1=expT1_b[bb][:, :],
                    op=ALU.mult),
                    reads=[r_xbcBC, r_expT1[bb]], writes=[r_cexp[bb]])
                for k in range(2):
                    p.op("gp", lambda e, k=k, bb=bb: e.tensor_tensor(
                        out=ttmp_b[bb][:, k * CH:(k + 1) * CH],
                        in0=t1sb_b[bb][:, :],
                        in1=pen[:, k * CH:(k + 1) * CH], op=ALU.add),
                        reads=[r_t1sb[bb], r_pen], writes=[r_ttmp[bb][k]])
                    p.op("sc", lambda e, k=k, h=h, bb=bb: e.activation(
                        out=m0_b[bb][:, k * CH:(k + 1) * CH],
                        in_=ttmp_b[bb][:, k * CH:(k + 1) * CH], func=AF.Exp,
                        bias=tts[:, k * 64 + h:k * 64 + h + 1], scale=1.0),
                        reads=[r_ttmp[bb][k], r_nAT], writes=[r_m0[bb][k]])
                    p.op("ve", lambda e, k=k, h=h, bb=bb:
                         e.scalar_tensor_tensor(
                        out=mh_b[bb][:, k * CH:(k + 1) * CH],
                        in0=G2sb[:, k * CH:(k + 1) * CH],
                        scalar=tts[:, 128 + k * 64 + h:128 + k * 64 + h + 1],
                        in1=m0_b[bb][:, k * CH:(k + 1) * CH],
                        op0=ALU.mult, op1=ALU.mult),
                        reads=[r_G2, r_dtT, r_m0[bb][k]],
                        writes=[r_mh[bb]] if k == 0 else (),
                        rmw=[r_mh[bb]] if k else ())
                    p.op("ve", lambda e, k=k, h=h, bb=bb:
                         e.scalar_tensor_tensor(
                        out=mh_b[bb][:, k * CH:(k + 1) * CH],
                        in0=identD[:, k * CH:(k + 1) * CH],
                        scalar=dpf[:, h:h + 1],
                        in1=mh_b[bb][:, k * CH:(k + 1) * CH],
                        op0=ALU.mult, op1=ALU.add),
                        reads=[r_identD, r_dpf], rmw=[r_mh[bb]])
                if h + 1 < H:
                    t1(h + 1)
                pairbank = 2 + ((h // 2) % 2)
                rb = (h % 2) * 64
                mm = [lambda e, h=h, k=k, pairbank=pairbank, rb=rb, bb=bb:
                      e.matmul(
                    pb[pairbank][rb:rb + 64, 0:CH],
                    rawXT[:, k * DIN + h * HD:k * DIN + (h + 1) * HD],
                    mh_b[bb][:, k * CH:(k + 1) * CH],
                    start=(k == 0), stop=False) for k in range(2)]
                mm.append(lambda e, h=h, pairbank=pairbank, rb=rb, bb=bb:
                          e.matmul(
                    pb[pairbank][rb:rb + 64, 0:CH],
                    S[:, h * HD:(h + 1) * HD],
                    cexp_b[bb][:, 0:CH],
                    start=False, stop=True))
                p.op("te", mm, reads=[r_XT, r_mh[bb], r_S, r_cexp[bb]],
                     writes=[pbres[pairbank]] if h % 2 == 0 else (),
                     rmw=[pbres[pairbank]] if h % 2 else ())
                if h % 2 == 1:
                    pair = h // 2
                    ys = pair % 2
                    p.op("ve", lambda e, pair=pair, pairbank=pairbank:
                         e.tensor_tensor(
                        out=xbc[:, pair * CH:(pair + 1) * CH],
                        in0=pb[pairbank][:, 0:CH],
                        in1=xbc[:, pair * CH:(pair + 1) * CH], op=ALU.mult),
                        reads=[pbres[pairbank]], rmw=[r_xbcx])
                    p.op("sc", lambda e, pair=pair, ys=ys: e.square(
                        out=ysq[:, ys * CH:(ys + 1) * CH],
                        in_=xbc[:, pair * CH:(pair + 1) * CH]),
                        reads=[r_xbcx], writes=[r_ysq[ys]])
                    p.op("te", lambda e, pair=pair, ys=ys: e.matmul(
                        pb[6][0:1, 0:CH], ones[:, 0:1],
                        ysq[:, ys * CH:(ys + 1) * CH],
                        start=(pair == 0), stop=(pair == 31)),
                        reads=[r_ysq[ys], r_ones],
                        writes=[pbres[6]] if pair == 0 else (),
                        rmw=[pbres[6]] if pair else ())

        def emit_rms_outproj(ci):
            p.op("sc", lambda e: e.copy(out=rows[0:1, 0:CH],
                                        in_=pb[6][0:1, 0:CH]),
                 reads=[pbres[6]], rmw=[r_rows])
            p.op("ve", lambda e: e.tensor_scalar(
                out=rows[0:1, 0:CH], in0=rows[0:1, 0:CH],
                scalar1=1.0 / DIN, scalar2=None, op0=ALU.mult), rmw=[r_rows])
            p.op("sc", lambda e: e.activation(
                out=rows[0:1, 4 * CH:5 * CH], in_=rows[0:1, 0:CH],
                func=AF.Sqrt, bias=csm[0:1, 1:2], scale=1.0),
                reads=[r_csm], rmw=[r_rows])
            p.op("ve", lambda e: e.reciprocal(
                out=rows[0:1, CH:2 * CH], in_=rows[0:1, 4 * CH:5 * CH]),
                rmw=[r_rows])
            p.op("te", lambda e: e.matmul(
                pb[7][:, 0:CH], ones[0:1, 0:128], rows[0:1, CH:2 * CH],
                start=True, stop=True),
                reads=[r_rows, r_ones], writes=[pbres[7]])
            for f in range(ZCC):
                p.op("ve", lambda e, f=f: e.scalar_tensor_tensor(
                    out=xbc[:, f * CH:(f + 1) * CH],
                    in0=xbc[:, f * CH:(f + 1) * CH],
                    scalar=normw[:, f:f + 1],
                    in1=pb[7][:, 0:CH], op0=ALU.mult, op1=ALU.mult),
                    reads=[pbres[7], r_normw], rmw=[r_xbcx])
            for m in range(FD):
                bank = m % 2
                for half in range(2):
                    wb = (m * 2 + half) % 2
                    load_w_slice(wb * FD * 128,
                                 woutr[:, half * 16:(half + 1) * 16,
                                       m * 128:(m + 1) * 128],
                                 r_wbuf[wb], 16, 128, wbuf, "w%d" % wb)
                    mm = []
                    for kk in range(16):
                        k = half * 16 + kk
                        mm.append(lambda e, k=k, kk=kk, wb=wb, bank=bank:
                                  e.matmul(
                            pb[bank][:, 0:CH],
                            wbuf[:, wb * FD * 128 + kk * 128:
                                 wb * FD * 128 + (kk + 1) * 128],
                            xbc[:, k * CH:(k + 1) * CH],
                            start=(k == 0), stop=(k == 31)))
                    p.op("te", mm, reads=[r_wbuf[wb], r_xbcx],
                         writes=[pbres[bank]] if half == 0 else (),
                         rmw=[pbres[bank]] if half else ())
                p.op("ve", lambda e, m=m, bank=bank: e.tensor_tensor(
                    out=u_c[:, m * CH:(m + 1) * CH],
                    in0=pb[bank][:, 0:CH],
                    in1=u_c[:, m * CH:(m + 1) * CH], op=ALU.add),
                    reads=[pbres[bank]], rmw=[r_uc])
                p.op("gp", lambda e, m=m, ci=ci: e.dma_start(
                    out=d_out[m * 128:(m + 1) * 128,
                              (ci - OWN0) * CH:(ci - OWN0 + 1) * CH],
                    in_=u_c[:, m * CH:(m + 1) * CH]),
                    reads=[r_uc], dma="st")

        def emit_states(ci):
            for h in range(H):
                bb = h % 2
                for k in range(2):
                    p.op("ve", lambda e, h=h, k=k, bb=bb: e.tensor_scalar(
                        out=Xs_b[bb][:, k * 64:(k + 1) * 64],
                        in0=rawXT[:, k * DIN + h * HD:k * DIN + (h + 1) * HD],
                        scalar1=tts[:, 256 + k * 64 + h:256 + k * 64 + h + 1],
                        scalar2=None, op0=ALU.mult),
                        reads=[r_XT, r_dtdsT], writes=[r_Xs[bb]] if k == 0
                        else (), rmw=[r_Xs[bb]] if k else ())
                bank = h // 8
                col = (h % 8) * 64
                mm = [lambda e, h=h, k=k, bank=bank, col=col, bb=bb: e.matmul(
                    pb[bank][:, col:col + 64],
                    BT[:, k * 128:(k + 1) * 128],
                    Xs_b[bb][:, k * 64:(k + 1) * 64],
                    start=(k == 0), stop=(k == 1)) for k in range(2)]
                p.op("te", mm, reads=[r_Xs[bb], r_BT, r_XT],
                     writes=[pbres[bank]] if h % 8 == 0 else (),
                     rmw=[pbres[bank]] if h % 8 else ())

        def emit_ebT(ci):
            p.op("te", lambda e: e.transpose(
                pb[5][0:1, 0:64], ecol[:, 1:2], id128[0:64, 0:64]),
                reads=[r_ecol, r_id], writes=[pbres[5]])
            p.op("sc", lambda e: e.copy(out=erow[0:1, :],
                                        in_=pb[5][0:1, 0:64]),
                 reads=[pbres[5]], writes=[r_erow])
            p.op("te", lambda e: e.matmul(
                pb[5][:, 0:64], ones[0:1, 0:128], erow[0:1, :],
                start=True, stop=True),
                reads=[r_erow, r_ones], writes=[pbres[5]])
            p.op("sc", lambda e: e.copy(out=ebT[:, :], in_=pb[5][:, 0:64]),
                 reads=[pbres[5]], writes=[r_ebT])

        def emit_supdate(ci):
            for b in range(8):
                p.op("gp", lambda e, b=b: e.tensor_tensor(
                    out=S[:, b * 512:(b + 1) * 512].rearrange(
                        "p (h x) -> p h x", h=8),
                    in0=S[:, b * 512:(b + 1) * 512].rearrange(
                        "p (h x) -> p h x", h=8),
                    in1=ebT[:, b * 8:(b + 1) * 8].rearrange(
                        "p (h o) -> p h o", o=1).to_broadcast([128, 8, 64]),
                    op=ALU.mult),
                    reads=[r_ebT], rmw=[r_S])
                p.op("ve", lambda e, b=b, ci=ci: e.scalar_tensor_tensor(
                    out=S[:, b * 512:(b + 1) * 512],
                    in0=pb[b][:, 0:512], scalar=valid[:, ci:ci + 1],
                    in1=S[:, b * 512:(b + 1) * 512],
                    op0=ALU.mult, op1=ALU.add),
                    reads=[pbres[b], r_valid], rmw=[r_S])
            if TAPS and ci >= OWN0:
                p.op("gp", [lambda e, ci=ci: e.dma_start(
                    out=d_taps["S%d" % ci], in_=S[:, :])],
                    reads=[r_S], dma="tap")
                p.op("gp", [lambda e, ci=ci: e.dma_start(
                    out=d_taps["e%d" % ci], in_=ebT[:, :])],
                    reads=[r_ebT], dma="tap")

        block = ctx.enter_context(nc.Block())
        for ci in range(NCH):
            own = ci >= OWN0
            emit_ln(ci, lambda f: xn_c[:, f * CH:(f + 1) * CH], r_xnc,
                    u_c, r_uc)
            mlist = [(8448, 64, "dt", 0),
                     (4096 + 128 * 32, 128, "raw", 32),
                     (4096 + 128 * 33, 128, "raw", 33)]
            mlist += [(4096 + 128 * cc, 128, "raw", cc) for cc in range(32)]
            emit_inproj(ci, mlist, xn_c, r_xnc)
            emit_dt(ci, own)
            emit_conv(ci, 34)
            emit_transposes(ci, own)
            if own:
                emit_G2(ci)
                emit_inproj(ci, [(128 * zc, 128, "z", zc)
                                 for zc in range(ZCC)], xn_c, r_xnc)
                emit_Y(ci)
                emit_rms_outproj(ci)
            if ci < NCH - 1:
                emit_ebT(ci)
                emit_states(ci)
                emit_supdate(ci)
            p.flush(block)

        p.fence()
        o4 = 0

        def nxt4(cols):
            nonlocal o4
            a = o4
            o4 += cols
            return a
        O_XN4 = nxt4(FD * TOK)
        O_ACC = nxt4(FD * TOK)
        O_HB = nxt4(JCC * TOK)
        O_SG = nxt4(TOK)
        O_CEB = nxt4(TOK)
        O_CET = nxt4(TOK)
        O_RT = nxt4(640)
        O_CEST = nxt4(TOK)
        O_WB4 = nxt4(2 * FD * 128)
        assert o4 <= AR_COLS, o4

        xn = seg(O_XN4, FD * TOK)
        acc = seg(O_ACC, FD * TOK)
        hbuf = seg(O_HB, JCC * TOK)
        u4 = seg(O_HB, FD * CH)
        sg = seg(O_SG, TOK)
        ceB = seg(O_CEB, TOK)
        ceT = seg(O_CET, TOK)
        rt = seg(O_RT, 640)
        wb4 = seg(O_WB4, 2 * FD * 128)

        r_xn, r_acc, r_hb = Res(), Res(), Res()
        r_sg, r_ceB, r_ceT, r_cest = Res(), Res(), Res(), Res()
        r_ce = Res()
        r_probs, r_work, r_eq, r_amask = Res(), Res(), Res(), Res()
        r_m8, r_ren, r_cum, r_act, r_rw, r_col = (Res() for _ in range(6))
        r_wb4 = [Res(), Res()]

        CEA = 0
        PRB = 256
        WRK = PRB + 32
        EQ = WRK + 32
        AMK = EQ + 32
        M8 = AMK + 32
        REN = M8 + 8
        CUM = REN + 8
        ACT = CUM + 8
        RW = ACT + 8
        COLS = RW + 8

        for ci in range(OWN0, NCH):
            emit_ln(ci, lambda f, ci=ci: xn[:, f * TOK + (ci - OWN0) * CH:
                                            f * TOK + (ci - OWN0 + 1) * CH],
                    r_xn, u4, r_hb, usem="u4")

        for tc in range(8):
            mm = [lambda e, f=f, tc=tc: e.matmul(
                pb[4][:, 0:32],
                xn[:, f * TOK + tc * 128:f * TOK + (tc + 1) * 128],
                gwT[:, f * 32:(f + 1) * 32],
                start=(f == 0), stop=(f == FD - 1)) for f in range(FD)]
            p.op("te", mm, reads=[r_xn, r_gwT], writes=[pbres[4]])
            p.op("ve", lambda e: e.tensor_copy(out=rt[:, PRB:PRB + 32],
                                               in_=pb[4][:, 0:32]),
                 reads=[pbres[4]], writes=[r_probs])
            p.op("ve", lambda e: e.reduce_max(
                out=rt[:, COLS:COLS + 1], in_=rt[:, PRB:PRB + 32], axis=AX.X),
                reads=[r_probs], writes=[r_col])
            p.op("ve", lambda e: e.tensor_scalar(
                out=rt[:, PRB:PRB + 32], in0=rt[:, PRB:PRB + 32],
                scalar1=rt[:, COLS:COLS + 1], scalar2=None, op0=ALU.subtract),
                reads=[r_col], rmw=[r_probs])
            p.op("sc", lambda e: e.activation(
                out=rt[:, WRK:WRK + 32], in_=rt[:, PRB:PRB + 32],
                func=AF.Exp), reads=[r_probs], writes=[r_work])
            p.op("ve", lambda e: e.reduce_sum(
                out=rt[:, COLS:COLS + 1], in_=rt[:, WRK:WRK + 32], axis=AX.X),
                reads=[r_work], rmw=[r_col])
            p.op("ve", lambda e: e.reciprocal(
                out=rt[:, COLS + 1:COLS + 2], in_=rt[:, COLS:COLS + 1]),
                rmw=[r_col])
            p.op("ve", lambda e: e.tensor_scalar(
                out=rt[:, PRB:PRB + 32], in0=rt[:, WRK:WRK + 32],
                scalar1=rt[:, COLS + 1:COLS + 2], scalar2=None, op0=ALU.mult),
                reads=[r_work, r_col], rmw=[r_probs])
            p.op("ve", lambda e: e.max(out=rt[:, M8:M8 + 8],
                                       in_=rt[:, PRB:PRB + 32]),
                 reads=[r_probs], writes=[r_m8])
            p.op("ve", lambda e: e.reduce_sum(
                out=rt[:, COLS:COLS + 1], in_=rt[:, M8:M8 + 8], axis=AX.X),
                reads=[r_m8], rmw=[r_col])
            p.op("ve", lambda e: e.tensor_scalar(
                out=rt[:, COLS:COLS + 1], in0=rt[:, COLS:COLS + 1],
                scalar1=1e-12, scalar2=None, op0=ALU.max), rmw=[r_col])
            p.op("ve", lambda e: e.reciprocal(
                out=rt[:, COLS + 1:COLS + 2], in_=rt[:, COLS:COLS + 1]),
                rmw=[r_col])
            p.op("ve", lambda e: e.tensor_scalar(
                out=rt[:, REN:REN + 8], in0=rt[:, M8:M8 + 8],
                scalar1=rt[:, COLS + 1:COLS + 2], scalar2=None, op0=ALU.mult),
                reads=[r_m8, r_col], writes=[r_ren])
            p.op("ve", lambda e: e.tensor_tensor_scan(
                out=rt[:, CUM:CUM + 8], data0=ones[:, 0:8],
                data1=rt[:, REN:REN + 8], initial=0.0,
                op0=ALU.mult, op1=ALU.add),
                reads=[r_ren, r_ones], writes=[r_cum])
            p.op("ve", lambda e: e.memset(rt[:, ACT:ACT + 1], 0.0),
                 writes=[r_act])
            p.op("ve", lambda e: e.tensor_copy(
                out=rt[:, ACT + 1:ACT + 8], in_=rt[:, CUM:CUM + 7]),
                reads=[r_cum], rmw=[r_act])
            p.op("ve", lambda e: e.tensor_scalar(
                out=rt[:, ACT:ACT + 8], in0=rt[:, ACT:ACT + 8],
                scalar1=0.8, scalar2=None, op0=ALU.is_lt), rmw=[r_act])
            p.op("ve", lambda e: e.tensor_tensor(
                out=rt[:, RW:RW + 8], in0=rt[:, M8:M8 + 8],
                in1=rt[:, ACT:ACT + 8], op=ALU.mult),
                reads=[r_m8, r_act], writes=[r_rw])
            p.op("ve", lambda e: e.reduce_sum(
                out=rt[:, COLS:COLS + 1], in_=rt[:, RW:RW + 8], axis=AX.X),
                reads=[r_rw], rmw=[r_col])
            p.op("ve", lambda e: e.tensor_scalar(
                out=rt[:, COLS:COLS + 1], in0=rt[:, COLS:COLS + 1],
                scalar1=1e-12, scalar2=None, op0=ALU.max), rmw=[r_col])
            p.op("ve", lambda e: e.reciprocal(
                out=rt[:, COLS + 2:COLS + 3], in_=rt[:, COLS:COLS + 1]),
                rmw=[r_col])
            p.op("ve", lambda e: e.memset(rt[:, AMK:AMK + 32], 0.0),
                 writes=[r_amask])
            p.op("ve", lambda e: e.tensor_copy(out=rt[:, WRK:WRK + 32],
                                               in_=rt[:, PRB:PRB + 32]),
                 reads=[r_probs], writes=[r_work])
            for k in range(8):
                p.op("ve", lambda e, k=k: e.tensor_scalar(
                    out=rt[:, EQ:EQ + 32], in0=rt[:, WRK:WRK + 32],
                    scalar1=rt[:, M8 + k:M8 + k + 1], scalar2=None,
                    op0=ALU.is_equal),
                    reads=[r_work, r_m8], writes=[r_eq])
                p.op("ve", lambda e: e.tensor_tensor(
                    out=rt[:, EQ:EQ + 32], in0=rt[:, EQ:EQ + 32],
                    in1=iotam[:, :], op=ALU.mult),
                    reads=[r_iotam], rmw=[r_eq])
                p.op("ve", lambda e: e.tensor_reduce(
                    out=rt[:, COLS + 3:COLS + 4], in_=rt[:, EQ:EQ + 32],
                    axis=AX.X, op=ALU.min), reads=[r_eq], rmw=[r_col])
                p.op("ve", lambda e: e.tensor_scalar(
                    out=rt[:, EQ:EQ + 32], in0=iotam[:, :],
                    scalar1=rt[:, COLS + 3:COLS + 4], scalar2=None,
                    op0=ALU.is_equal),
                    reads=[r_iotam, r_col], rmw=[r_eq])
                p.op("ve", lambda e, k=k: e.scalar_tensor_tensor(
                    out=rt[:, AMK:AMK + 32], in0=rt[:, EQ:EQ + 32],
                    scalar=rt[:, ACT + k:ACT + k + 1],
                    in1=rt[:, AMK:AMK + 32], op0=ALU.mult, op1=ALU.add),
                    reads=[r_eq, r_act], rmw=[r_amask])
                p.op("ve", lambda e: e.scalar_tensor_tensor(
                    out=rt[:, WRK:WRK + 32], in0=rt[:, EQ:EQ + 32],
                    scalar=-2.0, in1=rt[:, WRK:WRK + 32],
                    op0=ALU.mult, op1=ALU.add),
                    reads=[r_eq], rmw=[r_work])
            p.op("ve", lambda e, tc=tc: e.scalar_tensor_tensor(
                out=rt[:, CEA + tc * 32:CEA + (tc + 1) * 32],
                in0=rt[:, PRB:PRB + 32], scalar=rt[:, COLS + 2:COLS + 3],
                in1=rt[:, AMK:AMK + 32], op0=ALU.mult, op1=ALU.mult),
                reads=[r_probs, r_col, r_amask],
                writes=[r_ce] if tc == 0 else (), rmw=[r_ce] if tc else ())
        for tc in range(8):
            bank = tc % 2
            p.op("te", lambda e, tc=tc, bank=bank: e.transpose(
                pb[bank][0:32, 0:128],
                rt[:, CEA + tc * 32:CEA + (tc + 1) * 32], id128[:, :]),
                reads=[r_ce, r_id], writes=[pbres[bank]])
            p.op("sc", lambda e, tc=tc, bank=bank: e.copy(
                out=ceT[0:32, tc * 128:(tc + 1) * 128],
                in_=pb[bank][0:32, 0:128]),
                reads=[pbres[bank]],
                writes=[r_ceT] if tc == 0 else (), rmw=[r_ceT] if tc else ())

        dmas = [lambda e, f=f: e.dma_start(
            out=acc[:, f * TOK:(f + 1) * TOK],
            in_=d_out[f * 128:(f + 1) * 128, :]) for f in range(FD)]
        p.op("sy", dmas, writes=[r_acc], dma="acc")

        cest = arena[0:1, O_CEST:O_CEST + TOK]

        for ex in range(NEXP):
            p.op("gp", lambda e, ex=ex: e.dma_start(
                out=cest, in_=ceT[ex:ex + 1, :]),
                reads=[r_ceT], writes=[r_cest], dma="ce")
            for th in range(2):
                p.op("te", lambda e, th=th: e.matmul(
                    pb[6][:, 0:512], ones[0:1, 0:128],
                    cest[:, th * 512:(th + 1) * 512], start=True, stop=True),
                    reads=[r_cest, r_ones], writes=[pbres[6]])
                p.op("ve", lambda e, th=th: e.tensor_copy(
                    out=ceB[:, th * 512:(th + 1) * 512], in_=pb[6][:, 0:512]),
                    reads=[pbres[6]],
                    writes=[r_ceB] if th == 0 else (),
                    rmw=[r_ceB] if th else ())
            for jc in range(JCC):
                for gu in range(2):
                    oc = jc + gu * 6
                    wb = oc % 2
                    p.op("sy", lambda e, ex=ex, oc=oc, wb=wb: e.dma_start(
                        out=wb4[:, wb * FD * 128:(wb + 1) * FD * 128]
                        .rearrange("p (k m) -> p k m", k=FD),
                        in_=d_wgu[ex:ex + 1, :, oc * 128:(oc + 1) * 128]
                        .rearrange("o (k p) m -> p (o k) m", p=128)),
                        writes=[r_wb4[wb]], dma="x%d" % wb)
                    for th in range(2):
                        bank = 2 * gu + th
                        mm = [lambda e, k=k, th=th, wb=wb, bank=bank:
                              e.matmul(
                            pb[bank][:, 0:512],
                            wb4[:, wb * FD * 128 + k * 128:
                                wb * FD * 128 + (k + 1) * 128],
                            xn[:, k * TOK + th * 512:k * TOK + (th + 1) * 512],
                            start=(k == 0), stop=(k == FD - 1))
                            for k in range(FD)]
                        p.op("te", mm, reads=[r_wb4[wb], r_xn],
                             writes=[pbres[bank]])
                        if gu == 0:
                            p.op("sc", lambda e, th=th, bank=bank:
                                 e.activation(
                                out=sg[:, th * 512:(th + 1) * 512],
                                in_=pb[bank][:, 0:512], func=AF.Silu),
                                reads=[pbres[bank]],
                                writes=[r_sg] if th == 0 else (),
                                rmw=[r_sg] if th else ())
                        else:
                            p.op("ve", lambda e, jc=jc, th=th, bank=bank:
                                 e.tensor_tensor(
                                out=hbuf[:, jc * TOK + th * 512:
                                         jc * TOK + (th + 1) * 512],
                                in0=pb[bank][:, 0:512],
                                in1=sg[:, th * 512:(th + 1) * 512],
                                op=ALU.mult),
                                reads=[pbres[bank], r_sg],
                                writes=[r_hb] if (jc == 0 and th == 0) and ex == 0
                                else (),
                                rmw=[r_hb] if not ((jc == 0 and th == 0)
                                                   and ex == 0) else ())
                            p.op("ve", lambda e, jc=jc, th=th:
                                 e.tensor_tensor(
                                out=hbuf[:, jc * TOK + th * 512:
                                         jc * TOK + (th + 1) * 512],
                                in0=hbuf[:, jc * TOK + th * 512:
                                         jc * TOK + (th + 1) * 512],
                                in1=ceB[:, th * 512:(th + 1) * 512],
                                op=ALU.mult),
                                reads=[r_ceB], rmw=[r_hb])
            for oc in range(FD):
                wb = oc % 2
                p.op("sy", lambda e, ex=ex, oc=oc, wb=wb: e.dma_start(
                    out=wb4[:, wb * FD * 128:wb * FD * 128 + JCC * 128]
                    .rearrange("p (k m) -> p k m", k=JCC),
                    in_=d_wd[ex:ex + 1, :, oc * 128:(oc + 1) * 128]
                    .rearrange("o (k p) m -> p (o k) m", p=128)),
                    writes=[r_wb4[wb]], dma="x%d" % wb)
                for th in range(2):
                    bank = 4 + th
                    mm = [lambda e, k=k, th=th, wb=wb, bank=bank: e.matmul(
                        pb[bank][:, 0:512],
                        wb4[:, wb * FD * 128 + k * 128:
                            wb * FD * 128 + (k + 1) * 128],
                        hbuf[:, k * TOK + th * 512:k * TOK + (th + 1) * 512],
                        start=(k == 0), stop=(k == JCC - 1))
                        for k in range(JCC)]
                    p.op("te", mm, reads=[r_wb4[wb], r_hb],
                         writes=[pbres[bank]])
                    p.op("ve", lambda e, oc=oc, th=th, bank=bank:
                         e.tensor_tensor(
                        out=acc[:, oc * TOK + th * 512:
                                oc * TOK + (th + 1) * 512],
                        in0=pb[bank][:, 0:512],
                        in1=acc[:, oc * TOK + th * 512:
                                oc * TOK + (th + 1) * 512], op=ALU.add),
                        reads=[pbres[bank]], rmw=[r_acc])
            p.flush(block)
        dmas = [lambda e, f=f: e.dma_start(
            out=d_out[f * 128:(f + 1) * 128, :],
            in_=acc[:, f * TOK:(f + 1) * TOK]) for f in range(FD)]
        p.op("gp", dmas, reads=[r_acc], dma="st")
        final = p.cnt["d_st"]
        p.raw("gp", lambda e, v=final: e.wait_ge(p.sem["d_st"], v))
        p.flush(block)
    return nc


_NC_CACHE = [None]


def kernel(L0_attn_u, L0_attn_W_in, L0_attn_conv_w, L0_attn_conv_b,
           L0_attn_dt_bias, L0_attn_A_log, L0_attn_Dp, L0_attn_norm_w,
           L0_attn_W_out, L0_ffn_gate_weight, L0_ffn_gate_up_proj,
           L0_ffn_down_proj):
    u = np.asarray(L0_attn_u, dtype=np.float32)
    w_in = np.ascontiguousarray(np.asarray(L0_attn_W_in, np.float32))
    w_out = np.ascontiguousarray(np.asarray(L0_attn_W_out, np.float32))
    conv_w = np.asarray(L0_attn_conv_w, np.float32)
    conv_b = np.asarray(L0_attn_conv_b, np.float32)
    dtb = np.ascontiguousarray(np.asarray(L0_attn_dt_bias,
                                          np.float32).reshape(64, 1))
    alog = np.ascontiguousarray(np.asarray(L0_attn_A_log,
                                           np.float32).reshape(64, 1))
    dp = np.asarray(L0_attn_Dp, np.float32)
    normw = np.asarray(L0_attn_norm_w, np.float32)
    gw = np.asarray(L0_ffn_gate_weight, np.float32)
    wgu = np.asarray(L0_ffn_gate_up_proj, np.float32)
    wd = np.asarray(L0_ffn_down_proj, np.float32)

    convw_l = np.ascontiguousarray(
        conv_w.reshape(CONVCC, 128, 4).transpose(1, 0, 2).reshape(128, -1))
    convb_l = np.ascontiguousarray(conv_b.reshape(CONVCC, 128).T)
    dpf_l = np.ascontiguousarray(np.broadcast_to(dp[None, :], (128, H)))
    normw_l = np.ascontiguousarray(normw.reshape(ZCC, 128).T)
    gwT_l = np.ascontiguousarray(gw.T)
    wgu_l = np.ascontiguousarray(wgu.transpose(0, 2, 1))
    wd_l = np.ascontiguousarray(wd.transpose(0, 2, 1))

    in_maps = []
    for c in range(8):
        b, j = divmod(c, 4)
        npad = 3072 - 1024 * j
        uT = np.concatenate([
            np.zeros((DM, npad), np.float32),
            np.ascontiguousarray(u[b, :1024 * j, :].T),
            np.ascontiguousarray(u[b, 1024 * j:1024 * (j + 1), :].T)], axis=1)
        vmask = np.zeros((NCH,), np.float32)
        vmask[OWN0 - 4 * j:] = 1.0
        valid_l = np.ascontiguousarray(
            np.broadcast_to(vmask[None, :], (128, NCH)))
        inval_l = np.ascontiguousarray(1.0 - valid_l)
        in_maps.append({
            "uT": np.ascontiguousarray(uT),
            "w_in": w_in, "w_out": w_out,
            "convw": convw_l, "convb": convb_l,
            "dtb": dtb, "alog": alog, "dpf": dpf_l, "normw": normw_l,
            "gwT": gwT_l, "wgu": wgu_l, "wd": wd_l,
            "valid": valid_l, "invalid": inval_l,
        })

    if _NC_CACHE[0] is None:
        _NC_CACHE[0] = build_program()
    nc = _NC_CACHE[0]
    res = run_bass_kernel_spmd(nc, in_maps, core_ids=list(range(8)))
    out = np.empty((2, 4096, DM), np.float32)
    for c in range(8):
        b, j = divmod(c, 4)
        out[b, 1024 * j:1024 * (j + 1), :] = np.asarray(
            res.results[c]["out"]).T
    return out
